# version 02
# baseline (speedup 1.0000x reference)
import math
from contextlib import ExitStack

import numpy as np
import ml_dtypes
import concourse.bass as bass
import concourse.bacc as bacc
import concourse.mybir as mybir
from concourse.bass import ds
from concourse.expressions import RuntimeValue
from concourse.bass_core_rust import IndirectOffsetOnAxis
from concourse.bass_utils import run_bass_kernel_spmd

F32 = mybir.dt.float32
BF16 = mybir.dt.bfloat16
U32 = mybir.dt.uint32
AF = mybir.ActivationFunctionType
ALU = mybir.AluOpType
AX = mybir.AxisListType
ET = mybir.EngineType

BF = ml_dtypes.bfloat16


def make_cfg(name="full"):
    c = {}
    c["name"] = name
    full = name == "full"
    c["B"] = 2
    c["S"] = 8192 if full else 4096
    c["D"] = 1024
    c["NCORE"] = 8
    c["NP"] = 8
    c["PCH"] = c["S"] // 32
    c["TPC"] = c["NP"] * c["PCH"]
    c["NT"] = c["TPC"] // 128
    c["NG_G"] = c["NCORE"] * c["NT"]
    c["T_TOT"] = c["B"] * c["S"]
    c["E"] = 64
    c["TOPK"] = 6
    c["CAP"] = 384 if full else 256
    c["EPC"] = c["E"] // c["NCORE"]
    c["FE"] = 512
    c["SHF"] = 2048
    c["H"] = 16
    c["KVH"] = 4
    c["HD"] = 64
    c["QL"] = 64
    c["KVL"] = 32
    c["KPAD"] = 3 * c["PCH"]
    c["KEYS"] = c["KPAD"] + c["S"]
    c["LB"] = c["E"] * c["CAP"] + c["TOPK"] * 128
    assert c["LB"] % 128 == 0
    c["EPS"] = 1e-5
    c["ROPE_BASE"] = 10000.0
    c["stages"] = ("ffn1", "attn", "ffn2")
    c["debug"] = ()
    return c


def glocal_of(cc, p):
    return 8 * (p // 2) + (cc if p % 2 == 0 else 7 - cc)


def chunks_of_core(c, cfg):
    b, cc = c // 4, c % 4
    return [32 * b + glocal_of(cc, p) for p in range(cfg["NP"])]


def core_token_rows(c, cfg):
    idx = []
    for g in chunks_of_core(c, cfg):
        idx.extend(range(g * cfg["PCH"], (g + 1) * cfg["PCH"]))
    return np.array(idx, dtype=np.int64)



def host_consts(cfg):
    E, CAP, TOPK, PCH = cfg["E"], cfg["CAP"], cfg["TOPK"], cfg["PCH"]
    NG_G = cfg["NG_G"]
    o = {}
    o["c_ident"] = np.eye(128, dtype=BF)
    u = np.triu(np.ones((128, 128), dtype=BF), 1)
    o["c_u128"] = u
    ug = np.triu(np.ones((NG_G, NG_G), dtype=BF), 1)
    o["c_ug"] = ug
    o["c_ones1"] = np.ones((128, 128), dtype=BF)
    o["c_ones1f"] = np.ones((128, 128), dtype=np.float32)
    o["c_onescf"] = np.ones((128, 1), dtype=np.float32)
    o["c_u128f"] = np.triu(np.ones((128, 128), dtype=np.float32), 1)
    o["c_iotaE"] = np.broadcast_to(np.arange(E, dtype=np.float32), (128, E)).copy()
    o["c_dec64"] = np.broadcast_to((E - np.arange(E)).astype(np.float32), (128, E)).copy()
    o["c_junk6"] = (E * CAP + np.arange(TOPK)[None, :] * 128
                    + np.arange(128)[:, None]).astype(np.float32)
    o["c_zero6"] = np.zeros((128, TOPK), np.float32)
    jj = np.arange(PCH)[None, :]
    kk = np.arange(128)[:, None]
    st1 = (kk + PCH - 256 <= jj).astype(BF)
    st2 = (kk + PCH - 128 <= jj).astype(BF)
    o["c_stmask"] = np.concatenate([st1] * 4 + [st2] * 4, 1)
    return o


def host_core_consts(c, cfg):
    NP, PCH, NT, HD = cfg["NP"], cfg["PCH"], cfg["NT"], cfg["HD"]
    o = {}
    agidx = (c * cfg["TPC"] + np.arange(cfg["TPC"])).astype(np.float32)
    o["k_agidx"] = agidx.reshape(NT, 128).T.copy()
    pos = np.zeros(cfg["TPC"])
    cc = c % 4
    for p in range(NP):
        g = glocal_of(cc, p)
        pos[p * PCH:(p + 1) * PCH] = g * PCH + np.arange(PCH)
    inv = 1.0 / cfg["ROPE_BASE"] ** (np.arange(0, HD, 2) / HD)
    ang = pos[None, :] * inv[:, None]
    cos_h = np.cos(ang)
    sin_h = np.sin(ang)
    o["k_cos"] = np.concatenate([cos_h] * 4, 0).astype(BF)
    o["k_sin"] = np.concatenate([sin_h] * 4, 0).astype(BF)
    ncols = sum(4 * PCH * (p + 1) // 128 for p in range(NP))
    val = np.ones((128, ncols), dtype=BF)
    colbase = 0
    for p in range(NP):
        L = 4 * PCH * (p + 1)
        g = glocal_of(cc, p)
        e_end = PCH * (g + 1)
        njunk = max(0, L - e_end)
        assert njunk % 128 == 0
        val[:, colbase:colbase + njunk // 128] = 0.0
        colbase += L // 128
    o["k_valcol"] = val
    return o


def host_weights(inputs, cfg, core):
    D, H, HD, KVH, QL, KVL = cfg["D"], cfg["H"], cfg["HD"], cfg["KVH"], cfg["QL"], cfg["KVL"]
    o = {}
    o["w_qa"] = np.ascontiguousarray(inputs["L0_attn_Wqa"])
    wqb = inputs["L0_attn_Wqb"]
    perm = np.zeros(D, dtype=np.int64)
    perm2 = np.zeros(D, dtype=np.int64)
    for j in range(H // 2):
        for dlt in range(2):
            h = 2 * j + dlt
            for f in range(HD // 2):
                perm[j * 128 + dlt * 32 + f] = h * HD + 2 * f
                perm[j * 128 + 64 + dlt * 32 + f] = h * HD + 2 * f + 1
                perm2[j * 128 + dlt * 32 + f] = h * HD + 2 * f + 1
                perm2[j * 128 + 64 + dlt * 32 + f] = h * HD + 2 * f
    o["w_qb"] = np.ascontiguousarray(
        np.concatenate([wqb[:, perm], wqb[:, perm2]], 1))
    o["w_kva"] = np.ascontiguousarray(inputs["L0_attn_Wkva"])
    wkvb = inputs["L0_attn_Wkvb"]
    colof = lambda kvh, d, kv: kvh * HD * 2 + d * 2 + kv
    permk = []
    permk2 = []
    for kc in range(KVH // 2):
        for xpart in range(2):
            for kvh in (2 * kc, 2 * kc + 1):
                for f in range(HD // 2):
                    permk.append(colof(kvh, 2 * f + xpart, 0))
                    permk2.append(colof(kvh, 2 * f + 1 - xpart, 0))
    permv = []
    for kvh in range(KVH):
        for d in range(HD):
            permv.append(colof(kvh, d, 1))
    o["w_kvb"] = np.ascontiguousarray(
        wkvb[:, permk + permv + permk2])
    o["w_out"] = np.ascontiguousarray(inputs["L0_attn_Wout"])
    o["w_bout"] = inputs["L0_attn_bout"].reshape(1, D)
    o["w_router"] = np.ascontiguousarray(inputs["L0_ffn_router_weight"].T)
    o["w_ebias"] = inputs["L0_ffn_expert_bias"].reshape(1, cfg["E"])
    o["w_sw1"] = np.ascontiguousarray(inputs["L0_ffn_sw1"].T)
    o["w_sw2"] = np.ascontiguousarray(inputs["L0_ffn_sw2"].T)
    e0 = core * cfg["EPC"]
    o["w_w1"] = np.ascontiguousarray(
        inputs["L0_ffn_w1"][e0:e0 + cfg["EPC"]])
    o["w_w2"] = np.ascontiguousarray(
        inputs["L0_ffn_w2"][e0:e0 + cfg["EPC"]])
    return o



class Em:
    COMPUTE = ("pe", "act", "dve", "pool")
    QUEUES = ("q_sp", "q_pool", "q_cc")

    def __init__(self, nc, ctx):
        self.nc = nc
        self.ctx = ctx
        self.sems = {}
        for n in ("c_pe", "c_act", "c_dve", "c_pool", "c_sp", "q_sp", "q_pool", "q_cc"):
            self.sems[n] = ctx.enter_context(nc.semaphore(n))
        self.cnt = {n: 0 for n in self.sems}
        self.q = {e: [] for e in ("pe", "act", "dve", "pool", "sp")}
        self.last_w = {}
        self.reads_since_w = {}
        self.waited = {e: {} for e in self.q}
        self.regs = {}
        self._chk = {}


    QOWNER = {"q_sp": "sp", "q_pool": "pool", "q_cc": "pool"}

    def _checkpoint(self, qsem, val):
        covered, tok = self._chk.get(qsem, (0, None))
        if covered >= val:
            return tok
        owner = self.QOWNER[qsem]
        total = self.cnt[qsem]
        csem = "c_" + owner
        self.cnt[csem] += 1
        cval = self.cnt[csem]
        qh, ch = self.sems[qsem], self.sems[csem]

        def f(h, qh=qh, total=total, ch=ch):
            h.wait_ge(qh, total).then_inc(ch, 1)

        self.q[owner].append(f)
        if self.waited[owner].get(qsem, 0) < total:
            self.waited[owner][qsem] = total
        tok = (csem, cval)
        self._chk[qsem] = (total, tok)
        return tok

    def _need_waits(self, eng, reads, writes):
        need = {}
        def add(sem, val):
            if sem.startswith("q_"):
                sem, val = self._checkpoint(sem, val)
            if val > need.get(sem, 0):
                need[sem] = val
        for rk in reads:
            w = self.last_w.get(rk)
            if w:
                add(*w)
        for rk in writes:
            w = self.last_w.get(rk)
            if w:
                add(*w)
            for r in self.reads_since_w.get(rk, ()):
                add(*r)
        out = []
        for sem, val in need.items():
            if eng == "pe" and sem == "c_pe":
                continue
            if self.waited[eng].get(sem, 0) >= val:
                continue
            out.append((sem, val))
            self.waited[eng][sem] = val
        return out

    def emit(self, eng, fn, reads=(), writes=(), dma=None, collective=False,
             skip_inc=False):
        waits = self._need_waits(eng, reads, writes)
        if dma is not None:
            sem = dma
            self.cnt[sem] += 16
            val = self.cnt[sem]
            inc = 16
        elif collective:
            sem = "q_cc"
            self.cnt[sem] += 1
            val = self.cnt[sem]
            inc = 1
        elif skip_inc:
            sem, val, inc = "c_" + eng, self.cnt["c_" + eng], 0
        else:
            sem = "c_" + eng
            self.cnt[sem] += 1
            val = self.cnt[sem]
            inc = 1
        semh = self.sems[sem]
        semobjs = self.sems

        def thunk(h, waits=waits, fn=fn, semh=semh, inc=inc):
            for (s, v) in waits:
                h.wait_ge(semobjs[s], v)
            inst = fn(h)
            if inc:
                inst.then_inc(semh, inc)

        self.q[eng].append(thunk)
        tok = (sem, val)
        for rk in writes:
            self.last_w[rk] = tok
            self.reads_since_w[rk] = []
        for rk in reads:
            self.reads_since_w.setdefault(rk, []).append(tok)
        return tok

    def wait(self, eng, sem, val):
        if self.waited[eng].get(sem, 0) >= val:
            return
        self.waited[eng][sem] = val
        semh = self.sems[sem]
        self.q[eng].append(lambda h, s=semh, v=val: h.wait_ge(s, v))

    def wait_region(self, eng, reads=(), writes=()):
        waits = self._need_waits(eng, reads, writes)
        for (s, v) in waits:
            semh = self.sems[s]
            self.q[eng].append(lambda h, ss=semh, vv=v: h.wait_ge(ss, vv))

    def raw(self, eng, fn):
        self.q[eng].append(fn)

    def barrier(self):
        snap = dict(self.cnt)
        for eng in ("pe", "act", "dve", "pool", "sp"):
            for sem, val in snap.items():
                if val > 0 and sem != "c_" + eng:
                    self.wait(eng, sem, val)

    def finalize(self, block):
        def run(h, lst):
            for f in lst:
                f(h)
        if self.q["sp"]:
            block.sync(lambda h: run(h, self.q["sp"]))
        if self.q["pool"]:
            block.gpsimd(lambda h: run(h, self.q["pool"]))
        if self.q["pe"]:
            block.tensor(lambda h: run(h, self.q["pe"]))
        if self.q["act"]:
            block.scalar(lambda h: run(h, self.q["act"]))
        if self.q["dve"]:
            block.vector(lambda h: run(h, self.q["dve"]))



def build_program(cfg):
    nc = bacc.Bacc("TRN2", target_bir_lowering=False, debug=False)
    D, E, TOPK, CAP = cfg["D"], cfg["E"], cfg["TOPK"], cfg["CAP"]
    NP, PCH, NT, NG_G = cfg["NP"], cfg["PCH"], cfg["NT"], cfg["NG_G"]
    TPC, T_TOT, LB = cfg["TPC"], cfg["T_TOT"], cfg["LB"]
    H, KVH, HD, QL, KVL = cfg["H"], cfg["KVH"], cfg["HD"], cfg["QL"], cfg["KVL"]
    FE, SHF, EPC = cfg["FE"], cfg["SHF"], cfg["EPC"]
    KPAD, KEYS = cfg["KPAD"], cfg["KEYS"]
    NB512 = TPC // 512
    stages = cfg["stages"]

    t_x = nc.dram_tensor("x_shard", [TPC, D], F32, kind="ExternalInput")
    tw = {}
    wshapes = {
        "w_qa": [D, QL], "w_qb": [QL, 2 * D], "w_kva": [D, KVL],
        "w_kvb": [KVL, 768],
        "w_out": [D, D], "w_bout": [1, D], "w_router": [D, E], "w_ebias": [1, E],
        "w_sw1": [D, SHF], "w_sw2": [SHF, D],
        "w_w1": [EPC, D, FE], "w_w2": [EPC, FE, D],
    }
    for k, s in wshapes.items():
        tw[k] = nc.dram_tensor(k, s, F32, kind="ExternalInput")
    NKT = sum(4 * PCH * (p + 1) // 128 for p in range(NP))
    cshapes = {
        "c_ident": (BF16, [128, 128]),
        "c_u128": (BF16, [128, 128]), "c_ug": (BF16, [NG_G, NG_G]),
        "c_ones1": (BF16, [128, 128]), "c_ones1f": (F32, [128, 128]),
        "c_onescf": (F32, [128, 1]), "c_u128f": (F32, [128, 128]),
        "c_iotaE": (F32, [128, E]), "c_dec64": (F32, [128, E]),
        "c_junk6": (F32, [128, TOPK]), "c_zero6": (F32, [128, TOPK]),
        "c_stmask": (BF16, [128, 8 * PCH]),
        "k_agidx": (F32, [128, NT]),
        "k_cos": (BF16, [128, TPC]), "k_sin": (BF16, [128, TPC]),
        "k_valcol": (BF16, [128, NKT]),
    }
    for k, (dt, s) in cshapes.items():
        tw[k] = nc.dram_tensor(k, s, dt, kind="ExternalInput")

    t_out = nc.dram_tensor("x_out", [TPC, D], F32, kind="ExternalOutput")

    dbg_drams = []
    def dram(name, shape, dt):
        t = nc.dram_tensor(name, shape, dt, kind="Internal")
        if name in cfg["debug"]:
            dbg_drams.append((name, t, shape, dt))
        return t
    d_agln_in = dram("agln_in", [TPC, D], BF16)
    d_agln_out = dram("agln_out", [T_TOT, D], BF16)
    d_agcnt_in = dram("agcnt_in", [NT, E], BF16)
    d_agcnt_out = dram("agcnt_out", [NG_G, E], BF16)
    d_bases = dram("basesbuf", [NG_G, E], F32)
    d_tilebufs = [dram("tilebuf%d" % i, [LB, 1], F32) for i in range(NT)]
    d_ar_in = dram("ar_in", [128, LB // 128], F32)
    d_ar_out = dram("ar_out", [LB, 1], F32)
    d_agy_in = dram("agy_in", [EPC * CAP, D], BF16)
    d_agy_out = dram("agy_out", [E * CAP, D], BF16)
    d_agk_in = dram("agk_in", [2 * KVH * 32, TPC], BF16)
    d_agk_out = dram("agk_out", [4 * 2 * KVH * 32, TPC], BF16)
    d_agv_in = dram("agv_in", [TPC, KVH * HD], BF16)
    d_agv_out = dram("agv_out", [4 * TPC, KVH * HD], BF16)
    NKTM = 4 * (PCH // 128) * NP
    d_kbufC = dram("kbufC", [KVH, KEYS // PCH, 64, PCH], BF16)
    d_vbufC = dram("vbufC", [KVH, KEYS // 128, 128, HD], BF16)
    d_kbufE = dram("kbufE", [KVH, 64, NKTM * 128], BF16)
    d_kbufO = dram("kbufO", [KVH, 64, NKTM * 128], BF16)
    d_vbufE = dram("vbufE", [KVH, 128, NKTM, HD], BF16)
    d_vbufO = dram("vbufO", [KVH, 128, NKTM, HD], BF16)

    ctx = ExitStack()
    with ctx:
        em = Em(nc, ctx)
        sb = lambda n, s, dt=F32: ctx.enter_context(nc.sbuf_tensor(n, s, dt))
        psum = lambda n, s, dt=F32: ctx.enter_context(nc.psum_tensor(n, s, dt))

        s_x = sb("s_x", [128, NT, D])
        s_lnfm = sb("s_lnfm", [128, D // 128, TPC // 2], BF16)
        s_lnroll = sb("s_lnroll", [128, 2, D], BF16)
        s_stats = sb("s_stats", [128, 2, NT])
        s_scr2 = sb("s_scr2", [128, 8])
        s_scores = sb("s_scores", [128, NT, E])
        s_mask = sb("s_mask", [128, NT, E])
        s_probs = s_scores
        s_pos = sb("s_pos", [128, E])
        s_top8 = sb("s_top8", [128, 8])
        s_e6 = sb("s_e6", [128, TOPK])
        s_pos6 = sb("s_pos6", [128, NT, TOPK])
        s_w6 = sb("s_w6", [128, NT, TOPK])
        s_slot6u = sb("s_slot6u", [128, NT, TOPK], U32)
        s_val6u = sb("s_val6u", [128, TOPK], U32)
        s_y6u = sb("s_y6u", [128, NT, TOPK], U32)
        s_cntg = sb("s_cntg", [NG_G, E], BF16)
        s_baserows = sb("s_baserows", [1, NT * E])
        s_idxf = sb("s_idxf", [128, 24])
        s_idxu = sb("s_idxu", [128, 24], U32)
        s_merge = sb("s_merge", [128, LB // 128])
        s_acc = sb("s_acc", [128, D])
        s_scr = s_acc
        s_ytile = sb("s_ytile", [128, D], BF16)
        s_wqa = sb("s_wqa", [128, D // 128, QL], BF16)
        s_wqb = sb("s_wqb", [QL, 2 * D], BF16)
        s_wkva = sb("s_wkva", [128, D // 128, KVL], BF16)
        s_wkvb = sb("s_wkvb", [KVL, 768], BF16)
        s_wr = sb("s_wr", [128, D // 128, E], BF16)
        NVON = (4 * PCH * NP // 128) * 65
        arA = sb("arA", [128, max(2 * 8192, 8192 + NVON + 2048 + 8 * PCH)], BF16)
        arB = sb("arB", [128, 16384], BF16)
        s_sw1 = arA[:, 0:8192].rearrange("p (a b) -> p a b", a=D // 128)
        s_sw2 = arA[:, 8192:16384].rearrange("p (a b) -> p a b", a=SHF // 128)
        s_hmid = arB[:, 0:8192].rearrange("p (a b) -> p a b", a=SHF // 128)
        s_w1e = arB[:, 8192:12288].rearrange("p (a b) -> p a b", a=D // 128)
        s_w2e = arB[:, 12288:16384].rearrange("p (a b) -> p a b", a=FE // 128)
        s_k = arA[:, 0:8192]
        s_von = arA[:, 8192:8192 + NVON]
        s_expw = arA[:, 8192 + NVON:8192 + NVON + 2048]
        s_cat = arA[:, 8192 + NVON + 2048:8192 + NVON + 2048 + 8 * PCH].rearrange(
            "p (a b) -> p a b", a=8)
        oq2 = 8 * PCH
        owout = oq2 + H * PCH
        oqa = owout + 2048
        okva = oqa + TPC
        orec = okva + H * PCH
        obcd = orec + 1024
        assert obcd + 2 * PCH <= 16384
        s_q = arB[:, 0:8 * PCH]
        s_q2 = arB[:, oq2:oq2 + H * PCH]
        s_wout = arB[:, owout:owout + 2048].rearrange(
            "p (a b) -> p a b", a=2)
        s_qa = arB[0:QL, oqa:oqa + TPC]
        s_kva = arB[0:KVL, okva:okva + TPC]
        s_catA = arB[0:64, okva:okva + H * PCH]
        s_rec = arB[:, orec:orec + 1024].bitcast(F32)
        s_bcd = arB[0:64, obcd:obcd + 2 * PCH]
        s_h1 = sb("s_h1", [128, FE // 128, CAP], BF16)
        s_xs = sb("s_xs", [128, D // 128, CAP], BF16)
        s_xstok = sb("s_xstok", [128, D], BF16)
        s_ident = sb("s_ident", [128, 128], BF16)
        s_ug = sb("s_ug", [NG_G, NG_G], BF16)
        s_ones1 = sb("s_ones1", [128, 128], BF16)
        s_ones1f = sb("s_ones1f", [128, 128])
        s_onescf = sb("s_onescf", [128, 1])
        s_u128f = sb("s_u128f", [128, 128])
        s_iotaE = sb("s_iotaE", [128, E])
        s_dec64 = sb("s_dec64", [128, E])
        s_junk6 = sb("s_junk6", [128, TOPK])
        s_zero6 = sb("s_zero6", [128, TOPK])
        s_stmask = sb("s_stmask", [128, 8 * PCH], BF16)
        s_agidx = sb("s_agidx", [128, NT])
        s_cos = sb("s_cos", [128, TPC], BF16)
        s_sin = sb("s_sin", [128, TPC], BF16)
        s_valcol = sb("s_valcol", [128, NKT], BF16)
        s_ebias_bc = sb("s_ebias_bc", [128, E])
        s_ebrow = arB[0:1, 1024:1024 + E]
        s_boutrow = arB[0:1, 0:D]
        s_bout_bc = sb("s_bout_bc", [128, D], BF16)
        s_zeros = sb("s_zeros", [128, 200])
        s_eps = sb("s_eps", [128, 1])
        s_oct = sb("s_oct", [128, 1])

        G = psum("G", [128, 2048])
        A1 = psum("A1", [128, 512])
        A2 = psum("A2", [128, 512])
        A3 = psum("A3", [128, 512])
        A4 = psum("A4", [128, 512])

        block = ctx.enter_context(nc.Block())

        def dma(eng, q, out, in_, reads, writes, **kw):
            return em.emit(eng, lambda h: h.dma_start(out=out, in_=in_, **kw),
                           reads=reads, writes=writes, dma=q)

        def setup_regs(eng, etype):
            def f(h):
                pid = nc.partition_id([etype])
                cc = pid % 4
                em.regs[eng] = {
                    "pid": pid, "cc": cc,
                    "gte": (pid // 4) * (32 * TPCH) + cc * TPCH,
                    "gto": (pid // 4) * (32 * TPCH) + (7 - cc) * TPCH,
                    "ivar": pid * (EPC * CAP),
                    "cc3": 3 - cc,
                    "kve128": cc * TPCH, "kvo128": (3 - cc) * TPCH,
                }
            em.raw(eng, f)

        def dyn_off(h, eng, terms, const, maxv=1 << 24):
            r = em.regs[eng]
            expr = None
            for regname, coef in terms:
                t = r[regname] * coef
                expr = t if expr is None else expr + t
            return expr + const

        const_loads = [
            ("c_ident", s_ident), ("c_u128f", s_u128f),
            ("c_ones1f", s_ones1f), ("c_onescf", s_onescf),
            ("c_ug", s_ug), ("c_ones1", s_ones1), ("c_iotaE", s_iotaE),
            ("c_dec64", s_dec64), ("c_junk6", s_junk6), ("c_zero6", s_zero6),
            ("c_stmask", s_stmask),
            ("k_agidx", s_agidx), ("k_cos", s_cos), ("k_sin", s_sin),
            ("k_valcol", s_valcol),
        ]
        for name, dst in const_loads:
            dma("sp", "q_sp", dst[:], tw[name].ap(), reads=(), writes=(name,))
        dma("pool", "q_pool", s_ebrow[:], tw["w_ebias"].ap(), (), ("w_ebrow",))
        dma("pool", "q_pool", s_boutrow[:], tw["w_bout"].ap(), (), ("w_brow",))
        dma("pool", "q_pool", s_wqa[:],
            tw["w_qa"].ap().rearrange("(a p) q -> p a q", p=128), (), ("w_qa",))
        dma("pool", "q_pool", s_wqb[:], tw["w_qb"].ap(), (), ("w_qb",))
        dma("pool", "q_pool", s_wkva[:],
            tw["w_kva"].ap().rearrange("(a p) q -> p a q", p=128), (), ("w_kva",))
        dma("pool", "q_pool", s_wkvb[:], tw["w_kvb"].ap(), (), ("w_kvb",))
        dma("pool", "q_pool", s_wr[:],
            tw["w_router"].ap().rearrange("(a p) q -> p a q", p=128), (), ("w_r",))
        dma("sp", "q_sp", s_x[:], t_x.ap().rearrange("(a p) d -> p a d", p=128),
            (), tuple("x%d" % i for i in range(NT)))
        em.emit("dve", lambda h: h.memset(s_zeros[:], 0.0), (), ("zeros",))
        em.emit("dve", lambda h: h.memset(s_eps[:], float(cfg["EPS"])), (), ("eps",))
        em.emit("dve", lambda h: h.memset(s_oct[:], float(HD) ** -0.5), (), ("oct",))

        em.emit("pe", lambda h: h.matmul(G[:, 0:E], s_ones1[0:1, :], s_ebrow[:],
                                         start=True, stop=True),
                reads=("c_ones1", "w_ebrow"), writes=("gb0",))
        em.emit("dve", lambda h: h.tensor_copy(out=s_ebias_bc[:], in_=G[:, 0:E]),
                reads=("gb0",), writes=("ebias_bc",))
        for half in range(2):
            em.emit("pe", lambda h, hf=half: h.matmul(
                G[:, 512 + hf * 512: 1024 + hf * 512], s_ones1[0:1, :],
                s_boutrow[0:1, hf * 512:(hf + 1) * 512], start=True, stop=True),
                reads=("c_ones1", "w_brow"), writes=("gb%d" % (1 + half),))
            em.emit("dve", lambda h, hf=half: h.tensor_copy(
                out=s_bout_bc[:, hf * 512:(hf + 1) * 512],
                in_=G[:, 512 + hf * 512:1024 + hf * 512]),
                reads=("gb%d" % (1 + half),), writes=("bout_bc",))

        def layer_norm_half(round_id, half, ag=True):
            t0, t1 = half * (NT // 2), (half + 1) * (NT // 2)
            for i in range(t0, t1):
                xt = s_x[:, i, :]
                em.emit("dve", lambda h, xt=xt, i=i: h.tensor_scalar(
                    out=s_scr[:, 0:D], in0=xt, scalar1=0.0, scalar2=None,
                    op0=ALU.add, op1=ALU.add, accum_out=s_scr2[:, 0:1]),
                    reads=("x%d" % i, "scr"), writes=("scr", "sumA"))
                em.emit("dve", lambda h, xt=xt: h.scalar_tensor_tensor(
                    out=s_scr[:, 0:D], in0=xt, scalar=1.0, in1=xt,
                    op0=ALU.mult, op1=ALU.mult, accum_out=s_scr2[:, 1:2]),
                    reads=("x%d" % i, "scr"), writes=("scr", "sumB"))
                em.emit("dve", lambda h, i=i: h.tensor_scalar(
                    out=s_stats[:, 0, i:i + 1], in0=s_scr2[:, 0:1],
                    scalar1=-1.0 / D, scalar2=None, op0=ALU.mult),
                    reads=("sumA",), writes=("negmean",))
                em.emit("dve", lambda h, i=i: h.scalar_tensor_tensor(
                    out=s_scr2[:, 2:3], in0=s_stats[:, 0, i:i + 1], scalar=1.0,
                    in1=s_stats[:, 0, i:i + 1], op0=ALU.mult, op1=ALU.mult),
                    reads=("negmean",), writes=("m2",))
                em.emit("dve", lambda h: h.scalar_tensor_tensor(
                    out=s_scr2[:, 3:4], in0=s_scr2[:, 1:2], scalar=1.0 / D,
                    in1=s_scr2[:, 2:3], op0=ALU.mult, op1=ALU.subtract),
                    reads=("sumB", "m2"), writes=("var",))
                em.emit("act", lambda h: h.activation(
                    out=s_scr2[:, 4:5], in_=s_scr2[:, 3:4], func=AF.Sqrt,
                    bias=s_eps[:, 0:1]), reads=("var", "eps"), writes=("std",))
                em.emit("dve", lambda h, i=i: h.reciprocal(
                    s_stats[:, 1, i:i + 1], s_scr2[:, 4:5]),
                    reads=("std",), writes=("rstd",))
                em.emit("dve", lambda h, xt=xt, i=i: h.tensor_scalar(
                    out=s_lnroll[:, i % 2, :], in0=xt,
                    scalar1=s_stats[:, 0, i:i + 1], scalar2=s_stats[:, 1, i:i + 1],
                    op0=ALU.add, op1=ALU.mult),
                    reads=("x%d" % i, "negmean", "rstd"),
                    writes=("lnroll%d" % (i % 2),))
                if ag:
                    dma("sp", "q_sp",
                        d_agln_in.ap()[i * 128:(i + 1) * 128, :],
                        s_lnroll[:, i % 2, :],
                        reads=("lnroll%d" % (i % 2),), writes=("aglnin",))
                for dch in range(D // 128):
                    em.emit("pe", lambda h, i=i, dch=dch: h.transpose(
                        A4[:].bitcast(BF16)[:, 0:128],
                        s_lnroll[:, i % 2, dch * 128:(dch + 1) * 128], s_ident[:]),
                        reads=("lnroll%d" % (i % 2), "c_ident"), writes=("pt",))
                    em.emit("dve", lambda h, i=i, dch=dch, t0=t0: h.tensor_copy(
                        out=s_lnfm[:, dch, (i - t0) * 128:(i - t0 + 1) * 128],
                        in_=A4[:].bitcast(BF16)[:, 0:128]),
                        reads=("pt",), writes=("lnfm",))

        Gb = lambda b, w=512: G[:, b * 512:b * 512 + w]
        TPCH = PCH // 128
        NHALF = NT // 2
        NDCH = D // 128
        scr16 = s_scr[:].bitcast(BF16)

        def router_half(rnd, half):
            t0 = half * NHALF
            for i in range(t0, t0 + NHALF):
                for dch in range(NDCH):
                    em.emit("pe", lambda h, i=i, dch=dch, t0=t0: h.matmul(
                        Gb(0, E),
                        s_lnfm[:, dch, (i - t0) * 128:(i - t0 + 1) * 128],
                        s_wr[:, dch, :],
                        start=(dch == 0), stop=(dch == NDCH - 1)),
                        reads=("lnfm", "w_r"), writes=("G0",))
                em.emit("act", lambda h, i=i: h.activation(
                    out=s_scores[:, i, :], in_=Gb(0, E), func=AF.Sigmoid),
                    reads=("G0",), writes=("sc%d" % i,))

        def shared_half(rnd, half):
            t0 = half * NHALF
            nblk = NHALF // 4
            for blk in range(nblk):
                boff = blk * 512
                for fhalf in range(2):
                    dma("pool", "q_pool", s_sw1[:],
                        tw["w_sw1"].ap().rearrange("(a p) f -> p a f", p=128)
                        [:, :, fhalf * (SHF // 2):(fhalf + 1) * (SHF // 2)],
                        reads=(), writes=("sw1",))
                    for fc in range(SHF // 2 // 128):
                        for dch in range(NDCH):
                            em.emit("pe", lambda h, fc=fc, dch=dch, boff=boff:
                                    h.matmul(A1[:],
                                             s_sw1[:, dch, fc * 128:(fc + 1) * 128],
                                             s_lnfm[:, dch, boff:boff + 512],
                                             start=(dch == 0),
                                             stop=(dch == NDCH - 1)),
                                    reads=("sw1", "lnfm"), writes=("A1",))
                        em.emit("act", lambda h, fc=fc, fhalf=fhalf: h.activation(
                            out=s_hmid[:, fhalf * 8 + fc, :], in_=A1[:],
                            func=AF.Gelu_apprx_tanh),
                            reads=("A1",), writes=("hmid",))
                for dh in range(2):
                    dma("pool", "q_pool", s_sw2[:],
                        tw["w_sw2"].ap().rearrange("(a p) f -> p a f", p=128)
                        [:, :, dh * 512:(dh + 1) * 512],
                        reads=(), writes=("sw2",))
                    for ts_ in range(4):
                        xt = t0 + blk * 4 + ts_
                        for fc in range(SHF // 128):
                            em.emit("pe", lambda h, fc=fc, ts_=ts_: h.matmul(
                                A2[:], s_hmid[:, fc, ts_ * 128:(ts_ + 1) * 128],
                                s_sw2[:, fc, :],
                                start=(fc == 0), stop=(fc == SHF // 128 - 1)),
                                reads=("hmid", "sw2"), writes=("A2",))
                        em.emit("dve", lambda h, xt=xt, dh=dh:
                                h.scalar_tensor_tensor(
                                    out=s_x[:, xt, dh * 512:(dh + 1) * 512],
                                    in0=A2[:], scalar=0.5,
                                    in1=s_x[:, xt, dh * 512:(dh + 1) * 512],
                                    op0=ALU.mult, op1=ALU.add),
                                reads=("A2", "x%d" % xt), writes=("x%d" % xt,))

        def routing(rnd):
            for i in range(NT):
                em.emit("dve", lambda h, i=i: h.tensor_tensor(
                    out=s_scr[:, 0:E], in0=s_scores[:, i, :], in1=s_ebias_bc[:],
                    op=ALU.add), reads=("sc%d" % i, "ebias_bc", "scr"),
                    writes=("scr",))
                em.emit("dve", lambda h: h.max(out=s_top8[:], in_=s_scr[:, 0:E]),
                        reads=("scr",), writes=("top8",))
                em.emit("dve", lambda h, i=i: h.tensor_scalar(
                    out=s_mask[:, i, :], in0=s_scr[:, 0:E],
                    scalar1=s_top8[:, TOPK - 1:TOPK], scalar2=None,
                    op0=ALU.is_ge), reads=("scr", "top8"), writes=("mk%d" % i,))
                em.emit("dve", lambda h, i=i: h.scalar_tensor_tensor(
                    out=s_probs[:, i, :], in0=s_scores[:, i, :], scalar=1.0,
                    in1=s_mask[:, i, :], op0=ALU.mult, op1=ALU.mult,
                    accum_out=s_scr2[:, 0:1]),
                    reads=("sc%d" % i, "mk%d" % i), writes=("sc%d" % i, "s2a"))
                em.emit("dve", lambda h: h.reciprocal(
                    s_scr2[:, 1:2], s_scr2[:, 0:1]), reads=("s2a",),
                    writes=("s2b",))
                em.emit("dve", lambda h, i=i: h.tensor_scalar_mul(
                    s_probs[:, i, :], s_probs[:, i, :], s_scr2[:, 1:2]),
                    reads=("sc%d" % i, "s2b"), writes=("sc%d" % i,))
                em.emit("pe", lambda h, i=i: h.matmul(
                    Gb(1)[0:E, i:i + 1], s_mask[:, i, :], s_onescf[:],
                    start=True, stop=True),
                    reads=("mk%d" % i, "c_onescf"), writes=("G1",))
            em.emit("dve", lambda h: h.memset(scr16[:, 0:128], 0.0),
                    reads=("scr",), writes=("scr",))
            em.emit("dve", lambda h: h.tensor_copy(
                out=scr16[0:E, 0:NT], in_=Gb(1)[0:E, 0:NT]),
                reads=("G1", "scr"), writes=("scr",))
            em.emit("pe", lambda h: h.transpose(
                A4[:].bitcast(BF16)[:, 0:128], scr16[:, 0:128], s_ident[:]),
                reads=("scr", "c_ident"), writes=("pt",))
            em.emit("dve", lambda h: h.tensor_copy(
                out=scr16[:, 128:256][0:NT, 0:E],
                in_=A4[:].bitcast(BF16)[0:NT, 0:E]),
                reads=("pt", "scr"), writes=("scrT",))
            dma("sp", "q_sp", d_agcnt_in.ap(), scr16[:, 128:256][0:NT, 0:E],
                reads=("scrT",), writes=("agcntin",))

        def bases_and_pos(rnd):
            em.emit("pool", lambda h: h.collective_compute(
                "AllGather", ALU.bypass, replica_groups=[list(range(8))],
                ins=[d_agcnt_in.ap().opt()], outs=[d_agcnt_out.ap().opt()]),
                reads=("agcntin",), writes=("agcntout",), collective=True)
            for b in range(cfg["B"]):
                for p in range(NP):
                    for hh in range(TPCH):
                        base = (32 * b + 8 * (p // 2)) * TPCH + hh
                        if p % 2 == 0:
                            def f(h, base=base, b=b, p=p, hh=hh):
                                return h.dma_start(
                                    out=s_cntg[base:base + 3 * TPCH + 1:TPCH, :],
                                    in_=d_agcnt_out.ap().rearrange(
                                        "(r m) e -> r m e", m=NT)
                                    [4 * b:4 * b + 4, p * TPCH + hh, :])
                            em.emit("sp", f, reads=("agcntout",),
                                    writes=("cntg",), dma="q_sp")
                        else:
                            for cc2 in range(4):
                                dstr = base + (7 - cc2) * TPCH
                                def f(h, dstr=dstr, b=b, cc2=cc2, p=p, hh=hh):
                                    return h.dma_start(
                                        out=s_cntg[dstr:dstr + 1, :],
                                        in_=d_agcnt_out.ap().rearrange(
                                            "(r m) e -> r m e", m=NT)
                                        [4 * b + cc2, p * TPCH + hh, :][None, :]
                                        if False else
                                        d_agcnt_out.ap().rearrange(
                                            "(r m) e -> r m e", m=NT)
                                        [4 * b + cc2:4 * b + cc2 + 1,
                                         p * TPCH + hh, :])
                                em.emit("sp", f, reads=("agcntout",),
                                        writes=("cntg",), dma="q_sp")
            em.emit("pe", lambda h: h.matmul(
                Gb(2)[0:NG_G, 0:E], s_ug[:], s_cntg[:], start=True, stop=True),
                reads=("cntg", "c_ug"), writes=("G2",))
            em.emit("dve", lambda h: h.tensor_copy(
                out=s_scr[0:NG_G, 0:E], in_=Gb(2)[0:NG_G, 0:E]),
                reads=("G2", "scr"), writes=("scr",))
            dma("sp", "q_sp", d_bases.ap(), s_scr[0:NG_G, 0:E],
                reads=("scr",), writes=("basesd",))
            for i in range(NT):
                p_, hh = divmod(i, TPCH)
                sign = 1 if p_ % 2 == 0 else -1
                Cc = (8 * (p_ // 2) + (0 if sign > 0 else 7)) * TPCH + hh

                def f(h, p_=p_, hh=hh, sign=sign, i=i):
                    r = em.regs["sp"]
                    gvar = r["gte"] if sign > 0 else r["gto"]
                    cst = (8 * (p_ // 2)) * TPCH + hh
                    return h.dma_start(out=s_baserows[0:1, i * E:(i + 1) * E],
                                       in_=d_bases.ap()[cst:, :][ds(gvar, 1), :])
                em.emit("sp", f, reads=("basesd",), writes=("baserows",),
                        dma="q_sp")

        def pos_extract(rnd):
            for i in range(NT):
                em.emit("pe", lambda h, i=i: h.matmul(
                    Gb(3)[:, 0:E], s_u128f[:], s_mask[:, i, :],
                    start=True, stop=False),
                    reads=("mk%d" % i, "c_u128f"), writes=("G3",))
                em.emit("pe", lambda h, i=i: h.matmul(
                    Gb(3)[:, 0:E], s_ones1f[0:1, :],
                    s_baserows[0:1, i * E:(i + 1) * E],
                    start=False, stop=True),
                    reads=("baserows", "c_ones1f"), writes=("G3",))
                em.emit("dve", lambda h: h.tensor_copy(
                    out=s_pos[:], in_=Gb(3)[:, 0:E]), reads=("G3",),
                    writes=("pos",))
                em.emit("dve", lambda h, i=i: h.tensor_tensor(
                    out=s_scr[:, 0:E], in0=s_mask[:, i, :], in1=s_dec64[:],
                    op=ALU.mult), reads=("mk%d" % i, "c_dec64", "scr"),
                    writes=("scr",))
                em.emit("dve", lambda h: h.max(out=s_top8[:], in_=s_scr[:, 0:E]),
                        reads=("scr",), writes=("top8",))
                em.emit("dve", lambda h: h.tensor_scalar(
                    out=s_e6[:], in0=s_top8[:, 0:TOPK], scalar1=-1.0,
                    scalar2=float(E), op0=ALU.mult, op1=ALU.add),
                    reads=("top8",), writes=("e6",))
                for k in range(TOPK):
                    em.emit("dve", lambda h, k=k: h.tensor_scalar(
                        out=s_scr[:, 64:128], in0=s_iotaE[:],
                        scalar1=s_e6[:, k:k + 1], scalar2=None,
                        op0=ALU.is_equal),
                        reads=("e6", "c_iotaE", "scr"), writes=("scr",))
                    em.emit("dve", lambda h, i=i, k=k: h.scalar_tensor_tensor(
                        out=s_scr[:, 128:192], in0=s_pos[:], scalar=1.0,
                        in1=s_scr[:, 64:128], op0=ALU.mult, op1=ALU.mult,
                        accum_out=s_pos6[:, i, k:k + 1]),
                        reads=("pos", "scr"), writes=("scr", "p6%d" % i))
                    em.emit("dve", lambda h, i=i, k=k: h.scalar_tensor_tensor(
                        out=s_scr[:, 128:192], in0=s_probs[:, i, :], scalar=0.5,
                        in1=s_scr[:, 64:128], op0=ALU.mult, op1=ALU.mult,
                        accum_out=s_w6[:, i, k:k + 1]),
                        reads=("sc%d" % i, "scr"), writes=("scr", "w6%d" % i))
                em.emit("dve", lambda h, i=i: h.tensor_scalar(
                    out=s_scr[:, 192:192 + TOPK], in0=s_pos6[:, i, :],
                    scalar1=float(CAP), scalar2=None, op0=ALU.is_lt),
                    reads=("p6%d" % i, "scr"), writes=("scr", "val6")),
                em.emit("dve", lambda h, i=i: h.tensor_scalar(
                    out=s_val6u[:], in0=s_pos6[:, i, :],
                    scalar1=float(CAP), scalar2=None, op0=ALU.is_lt),
                    reads=("p6%d" % i,), writes=("val6u",)),
                em.emit("dve", lambda h, i=i: h.tensor_tensor(
                    out=s_w6[:, i, :], in0=s_w6[:, i, :],
                    in1=s_scr[:, 192:192 + TOPK], op=ALU.mult),
                    reads=("w6%d" % i, "val6"), writes=("w6%d" % i,))
                em.emit("dve", lambda h, i=i: h.scalar_tensor_tensor(
                    out=s_scr[:, 200:200 + TOPK], in0=s_e6[:],
                    scalar=float(CAP), in1=s_pos6[:, i, :],
                    op0=ALU.mult, op1=ALU.add),
                    reads=("e6", "p6%d" % i, "scr"), writes=("scr", "slotf")),
                em.emit("dve", lambda h: h.tensor_copy(
                    out=s_scr[:, 208:208 + TOPK], in_=s_junk6[:]),
                    reads=("c_junk6", "scr"), writes=("scr",))
                em.emit("dve", lambda h: h.copy_predicated(
                    out=s_scr[:, 208:208 + TOPK], mask=s_val6u[:],
                    data=s_scr[:, 200:200 + TOPK]),
                    reads=("val6u", "slotf", "scr"), writes=("scr",))
                em.emit("dve", lambda h, i=i: h.tensor_copy(
                    out=s_slot6u[:, i, :], in_=s_scr[:, 208:208 + TOPK]),
                    reads=("scr",), writes=("sl%d" % i,))
                em.emit("dve", lambda h: h.tensor_copy(
                    out=s_scr[:, 216:216 + TOPK], in_=s_zero6[:]),
                    reads=("c_zero6", "scr"), writes=("scr",))
                em.emit("dve", lambda h: h.copy_predicated(
                    out=s_scr[:, 216:216 + TOPK], mask=s_val6u[:],
                    data=s_scr[:, 200:200 + TOPK]),
                    reads=("val6u", "slotf", "scr"), writes=("scr",))
                em.emit("dve", lambda h, i=i: h.tensor_copy(
                    out=s_y6u[:, i, :], in_=s_scr[:, 216:216 + TOPK]),
                    reads=("scr",), writes=("y6%d" % i,))

        def scatter_pairs(rnd):
            for k in range(TOPK):
                for i in range(NT):
                    em.emit("pool", lambda h, i=i, k=k: h.indirect_dma_start(
                        out=d_tilebufs[i].ap(),
                        out_offset=IndirectOffsetOnAxis(s_slot6u[:, i, k:k + 1], 0),
                        in_=s_agidx[:, i:i + 1], in_offset=None),
                        reads=("sl%d" % i, "k_agidx", "tb%d" % i),
                        writes=("tb%d" % i,), dma="q_pool")
            W = LB // 128
            for i in range(NT):
                col = 256 * (i % 2)
                dma("sp", "q_sp", s_scr[:, col:col + W],
                    d_tilebufs[i].ap().rearrange("(p f) o -> p (f o)", p=128),
                    reads=("tb%d" % i, "scrm%d" % (i % 2)),
                    writes=("scrm%d" % (i % 2),))
                if i == 0:
                    em.emit("dve", lambda h, col=col: h.tensor_copy(
                        out=s_merge[:], in_=s_scr[:, col:col + W]),
                        reads=("scrm0",), writes=("mrg",))
                else:
                    em.emit("dve", lambda h, col=col: h.tensor_tensor(
                        out=s_merge[:], in0=s_merge[:],
                        in1=s_scr[:, col:col + W], op=ALU.add),
                        reads=("scrm%d" % (i % 2), "mrg"), writes=("mrg",))
            dma("sp", "q_sp", d_ar_in.ap(), s_merge[:], reads=("mrg",),
                writes=("arin",))
            em.emit("pool", lambda h: h.collective_compute(
                "AllReduce", ALU.add, replica_groups=[list(range(8))],
                ins=[d_ar_in.ap().opt()], outs=[d_ar_out.ap().rearrange("(p f) o -> p (f o)", p=128).opt()]),
                reads=("arin",), writes=("arout",), collective=True)

        def agln_collective(rnd):
            em.emit("pool", lambda h: h.collective_compute(
                "AllGather", ALU.bypass, replica_groups=[list(range(8))],
                ins=[d_agln_in.ap().opt()], outs=[d_agln_out.ap().opt()]),
                reads=("aglnin",), writes=("aglnout",), collective=True)

        def xs_experts(rnd):
            NST = CAP // 128
            for j in range(EPC):
                for st in range(NST):
                    def f(h, j=j, st=st):
                        ivar = em.regs["pool"]["ivar"]
                        cst = j * CAP + st * 128
                        return h.dma_start(
                            out=s_idxf[:, j * NST + st:j * NST + st + 1],
                            in_=d_ar_out.ap()[cst:, :][ds(ivar, 128), :])
                    em.emit("pool", f, reads=("arout",), writes=("idxf",),
                            dma="q_pool")
            em.emit("dve", lambda h: h.tensor_copy(
                out=s_idxu[:, 0:EPC * NST], in_=s_idxf[:, 0:EPC * NST]),
                reads=("idxf",), writes=("idxu",))
            for j in range(EPC):
                dma("pool", "q_pool", s_w1e[:],
                    tw["w_w1"].ap()[j].rearrange("(a p) f -> p a f", p=128),
                    reads=(), writes=("w1e",))
                dma("pool", "q_pool", s_w2e[:],
                    tw["w_w2"].ap()[j].rearrange("(a p) f -> p a f", p=128),
                    reads=(), writes=("w2e",))
                for st in range(NST):
                    em.emit("pool", lambda h, j=j, st=st: h.indirect_dma_start(
                        out=s_xstok[:], out_offset=None,
                        in_=d_agln_out.ap(),
                        in_offset=IndirectOffsetOnAxis(
                            s_idxu[:, j * NST + st:j * NST + st + 1], 0)),
                        reads=("idxu", "aglnout", "xstok"), writes=("xstok",),
                        dma="q_pool")
                    for dch in range(NDCH):
                        em.emit("pe", lambda h, dch=dch: h.transpose(
                            A4[:].bitcast(BF16)[:, 0:128],
                            s_xstok[:, dch * 128:(dch + 1) * 128], s_ident[:]),
                            reads=("xstok", "c_ident"), writes=("pt",))
                        em.emit("dve", lambda h, dch=dch, st=st: h.tensor_copy(
                            out=s_xs[:, dch, st * 128:(st + 1) * 128],
                            in_=A4[:].bitcast(BF16)[:, 0:128]),
                            reads=("pt",), writes=("xs",))
                for fc in range(FE // 128):
                    for dch in range(NDCH):
                        em.emit("pe", lambda h, fc=fc, dch=dch: h.matmul(
                            A1[:, 0:CAP], s_w1e[:, dch, fc * 128:(fc + 1) * 128],
                            s_xs[:, dch, :], start=(dch == 0),
                            stop=(dch == NDCH - 1)),
                            reads=("w1e", "xs"), writes=("A1",))
                    em.emit("act", lambda h, fc=fc: h.activation(
                        out=s_h1[:, fc, :], in_=A1[:, 0:CAP],
                        func=AF.Gelu_apprx_tanh), reads=("A1",), writes=("h1",))
                for st in range(NST):
                    for dh in range(2):
                        for fc in range(FE // 128):
                            em.emit("pe", lambda h, st=st, dh=dh, fc=fc: h.matmul(
                                A2[:], s_h1[:, fc, st * 128:(st + 1) * 128],
                                s_w2e[:, fc, dh * 512:(dh + 1) * 512],
                                start=(fc == 0), stop=(fc == FE // 128 - 1)),
                                reads=("h1", "w2e"), writes=("A2",))
                        em.emit("dve", lambda h, dh=dh: h.tensor_copy(
                            out=s_ytile[:, dh * 512:(dh + 1) * 512], in_=A2[:]),
                            reads=("A2",), writes=("ytile",))
                    dma("sp", "q_sp",
                        d_agy_in.ap()[j * CAP + st * 128:j * CAP + (st + 1) * 128, :],
                        s_ytile[:], reads=("ytile",), writes=("agyin",))

        def combine(rnd):
            em.emit("pool", lambda h: h.collective_compute(
                "AllGather", ALU.bypass, replica_groups=[list(range(8))],
                ins=[d_agy_in.ap().opt()], outs=[d_agy_out.ap().opt()]),
                reads=("agyin",), writes=("agyout",), collective=True)
            for i in range(NT):
                for k in range(TOPK):
                    buf = s_ytile if k % 2 == 0 else s_xstok
                    bufrk = "ytile" if k % 2 == 0 else "xstok"
                    em.emit("pool", lambda h, i=i, k=k, buf=buf:
                            h.indirect_dma_start(
                                out=buf[:], out_offset=None,
                                in_=d_agy_out.ap(),
                                in_offset=IndirectOffsetOnAxis(
                                    s_y6u[:, i, k:k + 1], 0)),
                            reads=("y6%d" % i, "agyout", bufrk), writes=(bufrk,),
                            dma="q_pool")
                    if k == 0:
                        em.emit("dve", lambda h, i=i, buf=buf: h.tensor_scalar_mul(
                            s_acc[:], buf[:], s_w6[:, i, 0:1]),
                            reads=(bufrk, "w6%d" % i, "scr"), writes=("scr",))
                    else:
                        em.emit("dve", lambda h, i=i, k=k, buf=buf:
                                h.scalar_tensor_tensor(
                                    out=s_acc[:], in0=buf[:],
                                    scalar=s_w6[:, i, k:k + 1], in1=s_acc[:],
                                    op0=ALU.mult, op1=ALU.add),
                                reads=(bufrk, "w6%d" % i, "scr"), writes=("scr",))
                em.emit("dve", lambda h, i=i: h.tensor_tensor(
                    out=s_x[:, i, :], in0=s_x[:, i, :], in1=s_acc[:],
                    op=ALU.add), reads=("x%d" % i, "scr"), writes=("x%d" % i,))

        def zero_tilebufs(rnd):
            for i in range(NT):
                dma("sp", "q_sp",
                    d_tilebufs[i].ap().rearrange("(p f) o -> p (f o)", p=128),
                    s_zeros[:, 0:LB // 128],
                    reads=("zeros", "tb%d" % i), writes=("tb%d" % i,))

        def ffn_round(rnd):
            if rnd == 0:
                em.wait_region("act", writes=("w_ebrow", "w_brow"))
                em.wait_region("pool", writes=("w_ebrow", "w_brow"))
            zero_tilebufs(rnd)
            for half in (0, 1):
                layer_norm_half(rnd, half)
                router_half(rnd, half)
                shared_half(rnd, half)
            agln_collective(rnd)
            routing(rnd)
            bases_and_pos(rnd)
            pos_extract(rnd)
            scatter_pairs(rnd)
            xs_experts(rnd)
            combine(rnd)

        KVC = KVH // 2
        GK = 2048 // (4 * PCH)
        W2 = 2 * PCH
        krst = s_ytile[:, 0:512]
        vst = s_ytile[:, 512:768]

        def attn_stagea_half(half):
            t0c = half * (TPC // 2)
            for blk in range(NHALF // 4):
                boff = blk * 512
                gcol = t0c + boff
                for dch in range(NDCH):
                    em.emit("pe", lambda h, dch=dch, boff=boff: h.matmul(
                        A1[0:QL, 0:512], s_wqa[:, dch, :],
                        s_lnfm[:, dch, boff:boff + 512],
                        start=(dch == 0), stop=(dch == NDCH - 1)),
                        reads=("w_qa", "lnfm"), writes=("A1",))
                em.emit("dve", lambda h, gcol=gcol: h.tensor_copy(
                    out=s_qa[:, gcol:gcol + 512], in_=A1[0:QL, 0:512]),
                    reads=("A1",), writes=("qa",))
                for dch in range(NDCH):
                    em.emit("pe", lambda h, dch=dch, boff=boff: h.matmul(
                        A2[0:KVL, 0:512], s_wkva[:, dch, :],
                        s_lnfm[:, dch, boff:boff + 512],
                        start=(dch == 0), stop=(dch == NDCH - 1)),
                        reads=("w_kva", "lnfm"), writes=("A2",))
                em.emit("dve", lambda h, gcol=gcol: h.tensor_copy(
                    out=s_kva[:, gcol:gcol + 512], in_=A2[0:KVL, 0:512]),
                    reads=("A2",), writes=("kvacat",))

        def kv_stageb():
            for b in range(TPC // 512):
                bc0 = b * 512
                for kc in range(KVC):
                    em.emit("pe", lambda h, kc=kc, bc0=bc0: h.matmul(
                        G[:, 0:512], s_wkvb[:, kc * 128:(kc + 1) * 128],
                        s_kva[:, bc0:bc0 + 512], start=True, stop=True),
                        reads=("w_kvb", "kvacat"), writes=("Gs0",))
                    em.emit("pe", lambda h, kc=kc, bc0=bc0: h.matmul(
                        G[:, 512:1024],
                        s_wkvb[:, 512 + kc * 128:512 + (kc + 1) * 128],
                        s_kva[:, bc0:bc0 + 512], start=True, stop=True),
                        reads=("w_kvb", "kvacat"), writes=("Gs1",))
                    em.emit("dve", lambda h, bc0=bc0: h.tensor_tensor(
                        out=s_scr[0:64, 0:512], in0=G[0:64, 0:512],
                        in1=s_cos[0:64, bc0:bc0 + 512], op=ALU.mult),
                        reads=("Gs0", "k_cos", "scr"), writes=("scr",))
                    em.emit("dve", lambda h, bc0=bc0: h.tensor_tensor(
                        out=s_scr[0:64, 512:1024], in0=G[0:64, 512:1024],
                        in1=s_sin[0:64, bc0:bc0 + 512], op=ALU.mult),
                        reads=("Gs1", "k_sin", "scr"), writes=("scr",))
                    em.emit("dve", lambda h: h.tensor_tensor(
                        out=krst[0:64, :], in0=s_scr[0:64, 0:512],
                        in1=s_scr[0:64, 512:1024], op=ALU.subtract),
                        reads=("scr",), writes=("ytile",))
                    em.emit("dve", lambda h, bc0=bc0: h.tensor_tensor(
                        out=s_scr[64:128, 0:512], in0=G[64:128, 512:1024],
                        in1=s_sin[64:128, bc0:bc0 + 512], op=ALU.mult),
                        reads=("Gs1", "k_sin", "scr"), writes=("scr",))
                    em.emit("dve", lambda h, bc0=bc0: h.tensor_tensor(
                        out=s_scr[64:128, 512:1024], in0=G[64:128, 0:512],
                        in1=s_cos[64:128, bc0:bc0 + 512], op=ALU.mult),
                        reads=("Gs0", "k_cos", "scr"), writes=("scr",))
                    em.emit("dve", lambda h: h.tensor_tensor(
                        out=krst[64:128, :], in0=s_scr[64:128, 0:512],
                        in1=s_scr[64:128, 512:1024], op=ALU.add),
                        reads=("scr",), writes=("ytile",))
                    for (sr, dr) in ((0, 2 * kc * 64), (32, (2 * kc + 1) * 64),
                                     (64, 2 * kc * 64 + 32),
                                     (96, (2 * kc + 1) * 64 + 32)):
                        dma("sp", "q_sp",
                            d_agk_in.ap()[dr:dr + 32, bc0:bc0 + 512],
                            krst[sr:sr + 32, :],
                            reads=("ytile",), writes=("agkin",))
            for i in range(NT):
                em.emit("pe", lambda h, i=i: h.matmul(
                    A1[:, 0:256], s_kva[:, i * 128:(i + 1) * 128],
                    s_wkvb[:, 256:512], start=True, stop=True),
                    reads=("kvacat", "w_kvb"), writes=("A1",))
                em.emit("act", lambda h: h.copy(out=vst[:], in_=A1[:, 0:256]),
                        reads=("A1",), writes=("ytile",))
                dma("sp", "q_sp", d_agv_in.ap()[i * 128:(i + 1) * 128, :],
                    vst[:], reads=("ytile",), writes=("agvin",))

        def attn_collectives():
            em.emit("pool", lambda h: h.collective_compute(
                "AllGather", ALU.bypass,
                replica_groups=[[0, 1, 2, 3], [4, 5, 6, 7]],
                ins=[d_agk_in.ap().opt()], outs=[d_agk_out.ap().opt()]),
                reads=("agkin",), writes=("agkout",), collective=True)
            em.emit("pool", lambda h: h.collective_compute(
                "AllGather", ALU.bypass,
                replica_groups=[[0, 1, 2, 3], [4, 5, 6, 7]],
                ins=[d_agv_in.ap().opt()], outs=[d_agv_out.ap().opt()]),
                reads=("agvin",), writes=("agvout",), collective=True)

        def kv_bufs():
            for a in range(KVH):
                dma("sp", "q_sp", d_kbufC.ap()[a][0:3, :, :],
                    s_zeros[:].bitcast(BF16)[:, 0:KPAD // 2],
                    reads=("zeros",), writes=("kbufC",))
                dma("sp", "q_sp", d_vbufC.ap()[a][0:KPAD // 128, :, :],
                    s_zeros[:].bitcast(BF16)[:, 0:KPAD // 2],
                    reads=("zeros",), writes=("vbufC",))
            for pp in range(NP):
                for rp in range(4):
                    g = glocal_of(rp, pp)
                    dma("sp", "q_sp", d_kbufC.ap()[:, 3 + g, :, :],
                        d_agk_out.ap()[rp * 256:(rp + 1) * 256,
                                       pp * PCH:(pp + 1) * PCH]
                        .rearrange("(a r) t -> a r t", a=KVH),
                        reads=("agkout",), writes=("kbufC",))
                    kt0 = (KPAD + g * PCH) // 128
                    for a in range(KVH):
                        dma("sp", "q_sp",
                            d_vbufC.ap()[a][kt0:kt0 + TPCH, :, :],
                            d_agv_out.ap()[rp * TPC + pp * PCH:
                                           rp * TPC + (pp + 1) * PCH,
                                           a * HD:(a + 1) * HD]
                            .rearrange("(kt kp) d -> kt kp d", kp=128),
                            reads=("agvout",), writes=("vbufC",))
            for a in range(KVH):
                for (dk, dv, rk, rv) in (
                        (d_kbufE, d_vbufE, "cc", "kve128"),
                        (d_kbufO, d_vbufO, "cc3", "kvo128")):
                    def fs(h, a=a, dk=dk, rk=rk):
                        sh = em.regs["sp"][rk]
                        return h.dma_start(
                            out=dk.ap()[a].rearrange(
                                "f (c t) -> c f t", t=PCH),
                            in_=d_kbufC.ap()[a][ds(sh, NKTM * 128 // PCH),
                                                :, :])
                    em.emit("sp", fs, reads=("kbufC",),
                            writes=("kbufE",), dma="q_sp")

                    def fv(h, a=a, dv=dv, rv=rv):
                        sh = em.regs["sp"][rv]
                        return h.dma_start(
                            out=dv.ap()[a].rearrange("kp kt d -> kt kp d"),
                            in_=d_vbufC.ap()[a][ds(sh, NKTM), :, :])
                    em.emit("sp", fv, reads=("vbufC",),
                            writes=("vbufE",), dma="q_sp")

        def attn_phase(p):
            pu = cfg.get("phase_upto", 99)
            T = 4 * TPCH * (p + 1)
            L = 128 * T
            pc0, pc1 = p * PCH, (p + 1) * PCH
            colbase = sum(4 * TPCH * (q + 1) for q in range(p))
            kbp = d_kbufE if p % 2 == 0 else d_kbufO
            vbp = d_vbufE if p % 2 == 0 else d_vbufO
            r2 = lambda ap: ap.rearrange("p (a b) -> p a b", a=2)
            bc2 = lambda ap: ap.rearrange("p (a b) -> p a b", a=1).to_broadcast(
                [64, 2, PCH])
            cosb0 = bc2(s_cos[0:64, pc0:pc1])
            sinb0 = bc2(s_sin[0:64, pc0:pc1])
            cosb1 = bc2(s_cos[64:128, pc0:pc1])
            sinb1 = bc2(s_sin[64:128, pc0:pc1])
            for cp in range(4):
                for (jj, off) in ((2 * cp, 0), (2 * cp + 1, PCH)):
                    em.emit("pe", lambda h, jj=jj, off=off: h.matmul(
                        A4[:, off:off + PCH],
                        s_wqb[:, jj * 128:(jj + 1) * 128],
                        s_qa[:, pc0:pc1], start=True, stop=True),
                        reads=("w_qb", "qa"), writes=("A4",))
                    em.emit("pe", lambda h, jj=jj, off=off: h.matmul(
                        A3[:, off:off + PCH],
                        s_wqb[:, D + jj * 128:D + (jj + 1) * 128],
                        s_qa[:, pc0:pc1], start=True, stop=True),
                        reads=("w_qb", "qa"), writes=("A3",))
                qc0 = 2 * cp * PCH
                em.emit("dve", lambda h: h.tensor_tensor(
                    out=r2(s_scr[0:64, 0:W2]), in0=r2(A4[0:64, 0:W2]),
                    in1=cosb0, op=ALU.mult),
                    reads=("A4", "k_cos", "scr"), writes=("scr",))
                em.emit("dve", lambda h: h.tensor_tensor(
                    out=r2(s_scr[0:64, 512:512 + W2]), in0=r2(A3[0:64, 0:W2]),
                    in1=sinb0, op=ALU.mult),
                    reads=("A3", "k_sin", "scr"), writes=("scr",))
                em.emit("dve", lambda h, qc0=qc0: h.tensor_tensor(
                    out=s_q[0:64, qc0:qc0 + W2], in0=s_scr[0:64, 0:W2],
                    in1=s_scr[0:64, 512:512 + W2], op=ALU.subtract),
                    reads=("scr",), writes=("q",))
                em.emit("dve", lambda h: h.tensor_tensor(
                    out=r2(s_scr[64:128, 0:W2]), in0=r2(A3[64:128, 0:W2]),
                    in1=sinb1, op=ALU.mult),
                    reads=("A3", "k_sin", "scr"), writes=("scr",))
                em.emit("dve", lambda h: h.tensor_tensor(
                    out=r2(s_scr[64:128, 512:512 + W2]), in0=r2(A4[64:128, 0:W2]),
                    in1=cosb1, op=ALU.mult),
                    reads=("A4", "k_cos", "scr"), writes=("scr",))
                em.emit("dve", lambda h, qc0=qc0: h.tensor_tensor(
                    out=s_q[64:128, qc0:qc0 + W2], in0=s_scr[64:128, 0:W2],
                    in1=s_scr[64:128, 512:512 + W2], op=ALU.add),
                    reads=("scr",), writes=("q",))
            if pu < 2:
                return
            sqv = s_q[:].rearrange("p (j t) -> p j t", t=PCH)
            q2v = s_q2[:].rearrange("p (hh t) -> p hh t", t=PCH)
            for (sr, dr, par) in ((0, 0, 0), (32, 0, 1), (64, 32, 0),
                                  (96, 32, 1)):
                dma("sp", "q_sp", q2v[dr:dr + 32, par::2, :],
                    sqv[sr:sr + 32, :, :], reads=("q",), writes=("q2",))
            dma("sp", "q_sp", s_q2[64:128, :], s_q2[0:64, :],
                reads=("q2",), writes=("q2",))
            if pu < 3:
                return
            von3 = s_von[:, 0:T * 65].rearrange("p (t c) -> p t c", c=65)
            for a in range(KVH):
                dma("sp", "q_sp", s_k[0:64, 0:L], kbp.ap()[a][:, 0:L],
                    reads=("kbufE",), writes=("ksb",))
                dma("sp", "q_sp", s_k[64:128, 0:L], kbp.ap()[a][:, 0:L],
                    reads=("kbufE",), writes=("ksb",))
                dma("pool", "q_pool", von3[:, :, 0:64],
                    vbp.ap()[a][:, 0:T, :],
                    reads=("vbufE",), writes=("von",))
                em.emit("dve", lambda h, colbase=colbase, T=T: h.tensor_copy(
                    out=von3[:, :, 64:65],
                    in_=s_valcol[:, colbase:colbase + T].rearrange(
                        "p (t c) -> p t c", c=1)),
                    reads=("k_valcol",), writes=("von",))
                if pu < 4:
                    continue
                for kt in range(T):
                    sl = (kt % 2) * 4 * PCH
                    ek = "expw%d" % (kt % 2)
                    for hp in (0, 1):
                        pcol = sl + 2 * hp * PCH
                        gkey = "Gs%d" % (pcol // 512)
                        ktc0 = kt * 128
                        qc = (4 * a + 2 * hp) * PCH
                        th = (pcol // 512) % 2
                        em.emit("pe", lambda h, pcol=pcol, th=th,
                                ktc0=ktc0, qc=qc: h.matmul(
                            G[:, pcol:pcol + W2],
                            s_k[64 * th:64 * th + 64, ktc0:ktc0 + 128],
                            s_q2[64 * th:64 * th + 64, qc:qc + W2],
                            start=True, stop=True,
                            tile_position=(64 * th, 0)),
                            reads=("ksb", "q2"), writes=(gkey,))
                    if pu < 5:
                        continue
                    gks = tuple(sorted({"Gs%d" % (sl // 512),
                                        "Gs%d" % ((sl + 4 * PCH - 1) // 512)}))
                    em.emit("act", lambda h, sl=sl: h.activation(
                        out=s_expw[:, sl:sl + 4 * PCH],
                        in_=G[:, sl:sl + 4 * PCH], func=AF.Exp,
                        scale=s_oct[:, 0:1]),
                        reads=gks + ("oct",), writes=(ek,))
                    if kt >= T - 2:
                        mi = (kt - (T - 2)) * 4 * PCH
                        em.emit("dve", lambda h, sl=sl, mi=mi: h.tensor_tensor(
                            out=s_expw[:, sl:sl + 4 * PCH],
                            in0=s_expw[:, sl:sl + 4 * PCH],
                            in1=s_stmask[:, mi:mi + 4 * PCH], op=ALU.mult),
                            reads=(ek, "c_stmask"), writes=(ek,))
                    if pu < 6:
                        continue
                    for hp in (0, 1):
                        Ap = A1 if hp == 0 else A2
                        apk = "A1" if hp == 0 else "A2"
                        ec0 = sl + hp * 2 * PCH
                        em.emit("pe", lambda h, Ap=Ap, kt=kt, ec0=ec0:
                                h.matmul(
                                    Ap[0:65, 0:W2], von3[:, kt, :],
                                    s_expw[:, ec0:ec0 + W2],
                                    start=(kt == 0), stop=(kt == T - 1)),
                                reads=("von", ek), writes=(apk,))
                if pu < 7:
                    continue
                for hp in (0, 1):
                    Ap = A1 if hp == 0 else A2
                    apk = "A1" if hp == 0 else "A2"
                    em.emit("dve", lambda h, Ap=Ap: h.reciprocal(
                        s_rec[64:65, 0:W2], Ap[64:65, 0:W2]),
                        reads=(apk,), writes=("rec",))
                    em.emit("pe", lambda h: h.matmul(
                        A3[0:64, 0:W2], s_ones1f[64:65, 0:64],
                        s_rec[64:65, 0:W2], start=True, stop=True,
                        tile_position=(64, 0)),
                        reads=("rec", "c_ones1f"), writes=("A3",))
                    em.emit("act", lambda h: h.copy(
                        out=s_bcd[:, 0:W2], in_=A3[0:64, 0:W2]),
                        reads=("A3",), writes=("bcd",))
                    cc0 = (4 * a + 2 * hp) * PCH
                    em.emit("dve", lambda h, Ap=Ap, cc0=cc0: h.tensor_tensor(
                        out=s_catA[:, cc0:cc0 + W2], in0=Ap[0:64, 0:W2],
                        in1=s_bcd[:, 0:W2], op=ALU.mult),
                        reads=(apk, "bcd"), writes=("kvacat",))
            if pu < 8:
                return
            catv = s_catA[:].rearrange("p (j e t) -> p j e t", e=2, t=PCH)
            dma("sp", "q_sp", s_cat[0:64, :, :], catv[:, :, 0, :],
                reads=("kvacat",), writes=("cat",))
            dma("sp", "q_sp", s_cat[64:128, :, :], catv[:, :, 1, :],
                reads=("kvacat",), writes=("cat",))
            for j in range(8):
                b = j % 2
                dma("pool", "q_pool", s_wout[:, b, :],
                    tw["w_out"].ap().rearrange("(a p) q -> p a q", p=128)
                    [:, j:j + 1, :],
                    reads=(), writes=("wout%d" % b,))
                for ts_ in range(TPCH):
                    for dh in range(2):
                        bk = ts_ * 2 + dh
                        em.emit("pe", lambda h, bk=bk, j=j, b=b, dh=dh,
                                ts_=ts_: h.matmul(
                            G[:, bk * 512:(bk + 1) * 512],
                            s_cat[:, j, ts_ * 128:(ts_ + 1) * 128],
                            s_wout[:, b, dh * 512:(dh + 1) * 512],
                            start=(j == 0), stop=(j == 7)),
                            reads=("cat", "wout%d" % b), writes=("Gs%d" % bk,))
            for ts_ in range(TPCH):
                xt = p * TPCH + ts_
                for dh in range(2):
                    bk = ts_ * 2 + dh
                    em.emit("dve", lambda h, bk=bk, dh=dh: h.tensor_tensor(
                        out=s_scr[:, 0:512], in0=G[:, bk * 512:(bk + 1) * 512],
                        in1=s_bout_bc[:, dh * 512:(dh + 1) * 512], op=ALU.add),
                        reads=("Gs%d" % bk, "bout_bc", "scr"), writes=("scr",))
                    em.emit("dve", lambda h, xt=xt, dh=dh: h.tensor_tensor(
                        out=s_x[:, xt, dh * 512:(dh + 1) * 512],
                        in0=s_x[:, xt, dh * 512:(dh + 1) * 512],
                        in1=s_scr[:, 0:512], op=ALU.add),
                        reads=("scr", "x%d" % xt), writes=("x%d" % xt,))

        def attn_round():
            upto = cfg.get("attn_upto", 99)
            for half in (0, 1):
                layer_norm_half(1, half, ag=False)
                attn_stagea_half(half)
            if upto >= 2:
                kv_stageb()
            if upto >= 3:
                attn_collectives()
            if upto >= 4:
                kv_bufs()
            for p in range(NP if upto >= 7 else max(0, upto - 4)):
                attn_phase(p)

        def dump_debug():
            specs = {
                "stats": ([128, 2 * NT], F32,
                          lambda: s_stats[:].rearrange("p a b -> p (a b)"),
                          ("negmean", "rstd")),
                "scores": ([128, NT * E], F32,
                           lambda: s_scores[:].rearrange("p a b -> p (a b)"),
                           ("scores",)),
                "mask": ([128, NT * E], F32,
                         lambda: s_mask[:].rearrange("p a b -> p (a b)"), ("mask",)),
                "probs": ([128, NT * E], F32,
                          lambda: s_probs[:].rearrange("p a b -> p (a b)"),
                          ("probs",)),
                "e6": ([128, TOPK], F32, lambda: s_e6[:], ("e6",)),
                "pos6": ([128, NT * TOPK], F32,
                         lambda: s_pos6[:].rearrange("p a b -> p (a b)"), ("pos6",)),
                "w6": ([128, NT * TOPK], F32,
                       lambda: s_w6[:].rearrange("p a b -> p (a b)"), ("w6",)),
                "cntg": ([NG_G, E], BF16, lambda: s_cntg[:], ("cntg",)),
                "baserows": ([1, NT * E], F32, lambda: s_baserows[:], ("baserows",)),
            }
            for name in cfg["debug"]:
                if name not in specs:
                    continue
                shape, dt, src, rds = specs[name]
                t = nc.dram_tensor("dbg_" + name, shape, dt, kind="ExternalOutput")
                dma("sp", "q_sp", t.ap(), src(), reads=rds, writes=("dbgout",))
            for name, t, shape, dt in dbg_drams:
                o = nc.dram_tensor("dbg_" + name, shape, dt, kind="ExternalOutput")
                em.barrier()
                dma("sp", "q_sp", o.ap(), t.ap(), reads=(), writes=("dbgout",))

        def store_out():
            dump_debug()
            dma("sp", "q_sp", t_out.ap().rearrange("(a p) d -> p a d", p=128),
                s_x[:], reads=tuple("x%d" % i for i in range(NT)), writes=("out",))
            em.wait("sp", "q_sp", em.cnt["q_sp"])
            em.wait("sp", "q_pool", em.cnt["q_pool"])
            em.wait("sp", "q_cc", em.cnt["q_cc"])
            em.wait("sp", "c_pe", em.cnt["c_pe"])
            em.wait("sp", "c_act", em.cnt["c_act"])
            em.wait("sp", "c_dve", em.cnt["c_dve"])
            em.wait("sp", "c_pool", em.cnt["c_pool"])

        setup_regs("sp", ET.SP)
        setup_regs("pool", ET.Pool)

        if not stages:
            for half in (0, 1):
                layer_norm_half(0, half)
        if "ffn1" in stages:
            ffn_round(0)
        if "attn" in stages:
            em.barrier()
            attn_round()
            em.barrier()
        if "ffn2" in stages:
            ffn_round(1)
        store_out()

        em.finalize(block)
    return nc



_CACHE = {}


def _get_program(cfg):
    key = cfg["name"] + str(cfg["stages"])
    if key not in _CACHE:
        nc = build_program(cfg)
        nc.compile()
        _CACHE[key] = nc
    return _CACHE[key]


def make_in_maps(inputs, cfg):
    consts = host_consts(cfg)
    xflat = np.ascontiguousarray(
        np.asarray(inputs["L0_attn_x"], np.float32).reshape(cfg["T_TOT"], cfg["D"]))
    in_maps = []
    for c in range(cfg["NCORE"]):
        m = {}
        m.update({k: np.ascontiguousarray(v) for k, v in consts.items()})
        m.update({k: np.ascontiguousarray(v)
                  for k, v in host_core_consts(c, cfg).items()})
        m.update({k: np.ascontiguousarray(v.astype(np.float32))
                  for k, v in host_weights(inputs, cfg, c).items()})
        m["x_shard"] = np.ascontiguousarray(xflat[core_token_rows(c, cfg)])
        in_maps.append(m)
    return in_maps


def assemble_out(results, cfg):
    out = np.zeros((cfg["T_TOT"], cfg["D"]), np.float32)
    for c in range(cfg["NCORE"]):
        out[core_token_rows(c, cfg)] = results[c]["x_out"]
    return out.reshape(cfg["B"], cfg["S"], cfg["D"])


def kernel(**inputs):
    cfg = make_cfg("full")
    nc = _get_program(cfg)
    in_maps = make_in_maps(inputs, cfg)
    res = run_bass_kernel_spmd(nc, in_maps, core_ids=list(range(cfg["NCORE"])))
    return assemble_out(res.results, cfg)


# version 03 (final)
# speedup vs baseline: 1.0429x; 1.0429x over previous
import math
from contextlib import ExitStack

import numpy as np
import ml_dtypes
import concourse.bass as bass
import concourse.bacc as bacc
import concourse.mybir as mybir
from concourse.bass import ds
from concourse.expressions import RuntimeValue
from concourse.bass_core_rust import IndirectOffsetOnAxis
from concourse.bass_utils import run_bass_kernel_spmd

F32 = mybir.dt.float32
BF16 = mybir.dt.bfloat16
U32 = mybir.dt.uint32
AF = mybir.ActivationFunctionType
ALU = mybir.AluOpType
AX = mybir.AxisListType
ET = mybir.EngineType

BF = ml_dtypes.bfloat16


def make_cfg(name="full"):
    c = {}
    c["name"] = name
    full = name == "full"
    c["B"] = 2
    c["S"] = 8192 if full else 4096
    c["D"] = 1024
    c["NCORE"] = 8
    c["NP"] = 8
    c["PCH"] = c["S"] // 32
    c["TPC"] = c["NP"] * c["PCH"]
    c["NT"] = c["TPC"] // 128
    c["NG_G"] = c["NCORE"] * c["NT"]
    c["T_TOT"] = c["B"] * c["S"]
    c["E"] = 64
    c["TOPK"] = 6
    c["CAP"] = 384 if full else 256
    c["EPC"] = c["E"] // c["NCORE"]
    c["FE"] = 512
    c["SHF"] = 2048
    c["H"] = 16
    c["KVH"] = 4
    c["HD"] = 64
    c["QL"] = 64
    c["KVL"] = 32
    c["KPAD"] = 3 * c["PCH"]
    c["KEYS"] = c["KPAD"] + c["S"]
    c["LB"] = c["E"] * c["CAP"] + c["TOPK"] * 128
    assert c["LB"] % 128 == 0
    c["EPS"] = 1e-5
    c["ROPE_BASE"] = 10000.0
    c["stages"] = ("ffn1", "attn", "ffn2")
    c["debug"] = ()
    return c


def glocal_of(cc, p):
    return 8 * (p // 2) + (cc if p % 2 == 0 else 7 - cc)


def chunks_of_core(c, cfg):
    b, cc = c // 4, c % 4
    return [32 * b + glocal_of(cc, p) for p in range(cfg["NP"])]


def core_token_rows(c, cfg):
    idx = []
    for g in chunks_of_core(c, cfg):
        idx.extend(range(g * cfg["PCH"], (g + 1) * cfg["PCH"]))
    return np.array(idx, dtype=np.int64)



def host_consts(cfg):
    E, CAP, TOPK, PCH = cfg["E"], cfg["CAP"], cfg["TOPK"], cfg["PCH"]
    NG_G = cfg["NG_G"]
    o = {}
    o["c_ident"] = np.eye(128, dtype=BF)
    u = np.triu(np.ones((128, 128), dtype=BF), 1)
    o["c_u128"] = u
    ug = np.triu(np.ones((NG_G, NG_G), dtype=BF), 1)
    o["c_ug"] = ug
    o["c_ones1"] = np.ones((128, 128), dtype=BF)
    o["c_ones1f"] = np.ones((128, 128), dtype=np.float32)
    o["c_onescf"] = np.ones((128, 1), dtype=np.float32)
    o["c_u128f"] = np.triu(np.ones((128, 128), dtype=np.float32), 1)
    o["c_iotaE"] = np.broadcast_to(np.arange(E, dtype=np.float32), (128, E)).copy()
    o["c_dec64"] = np.broadcast_to((E - np.arange(E)).astype(np.float32), (128, E)).copy()
    o["c_junk6"] = (E * CAP + np.arange(TOPK)[None, :] * 128
                    + np.arange(128)[:, None]).astype(np.float32)
    o["c_zero6"] = np.zeros((128, TOPK), np.float32)
    jj = np.arange(PCH)[None, :]
    kk = np.arange(128)[:, None]
    st1 = (kk + PCH - 256 <= jj).astype(BF)
    st2 = (kk + PCH - 128 <= jj).astype(BF)
    o["c_stmask"] = np.concatenate([st1] * 4 + [st2] * 4, 1)
    return o


def host_core_consts(c, cfg):
    NP, PCH, NT, HD = cfg["NP"], cfg["PCH"], cfg["NT"], cfg["HD"]
    o = {}
    agidx = (c * cfg["TPC"] + np.arange(cfg["TPC"])).astype(np.float32)
    o["k_agidx"] = agidx.reshape(NT, 128).T.copy()
    pos = np.zeros(cfg["TPC"])
    cc = c % 4
    for p in range(NP):
        g = glocal_of(cc, p)
        pos[p * PCH:(p + 1) * PCH] = g * PCH + np.arange(PCH)
    inv = 1.0 / cfg["ROPE_BASE"] ** (np.arange(0, HD, 2) / HD)
    ang = pos[None, :] * inv[:, None]
    cos_h = np.cos(ang)
    sin_h = np.sin(ang)
    o["k_cos"] = np.concatenate([cos_h] * 4, 0).astype(BF)
    o["k_sin"] = np.concatenate([sin_h] * 4, 0).astype(BF)
    ncols = sum(4 * PCH * (p + 1) // 128 for p in range(NP))
    val = np.ones((128, ncols), dtype=BF)
    colbase = 0
    for p in range(NP):
        L = 4 * PCH * (p + 1)
        g = glocal_of(cc, p)
        e_end = PCH * (g + 1)
        njunk = max(0, L - e_end)
        assert njunk % 128 == 0
        val[:, colbase:colbase + njunk // 128] = 0.0
        colbase += L // 128
    o["k_valcol"] = val
    return o


def host_weights(inputs, cfg, core):
    D, H, HD, KVH, QL, KVL = cfg["D"], cfg["H"], cfg["HD"], cfg["KVH"], cfg["QL"], cfg["KVL"]
    o = {}
    o["w_qa"] = np.ascontiguousarray(inputs["L0_attn_Wqa"])
    wqb = inputs["L0_attn_Wqb"]
    perm = np.zeros(D, dtype=np.int64)
    perm2 = np.zeros(D, dtype=np.int64)
    for j in range(H // 2):
        for dlt in range(2):
            h = 2 * j + dlt
            for f in range(HD // 2):
                perm[j * 128 + dlt * 32 + f] = h * HD + 2 * f
                perm[j * 128 + 64 + dlt * 32 + f] = h * HD + 2 * f + 1
                perm2[j * 128 + dlt * 32 + f] = h * HD + 2 * f + 1
                perm2[j * 128 + 64 + dlt * 32 + f] = h * HD + 2 * f
    o["w_qb"] = np.ascontiguousarray(
        np.concatenate([wqb[:, perm], wqb[:, perm2]], 1))
    o["w_kva"] = np.ascontiguousarray(inputs["L0_attn_Wkva"])
    wkvb = inputs["L0_attn_Wkvb"]
    colof = lambda kvh, d, kv: kvh * HD * 2 + d * 2 + kv
    permk = []
    permk2 = []
    for kc in range(KVH // 2):
        for xpart in range(2):
            for kvh in (2 * kc, 2 * kc + 1):
                for f in range(HD // 2):
                    permk.append(colof(kvh, 2 * f + xpart, 0))
                    permk2.append(colof(kvh, 2 * f + 1 - xpart, 0))
    permv = []
    for kvh in range(KVH):
        for d in range(HD):
            permv.append(colof(kvh, d, 1))
    o["w_kvb"] = np.ascontiguousarray(
        wkvb[:, permk + permv + permk2])
    o["w_out"] = np.ascontiguousarray(inputs["L0_attn_Wout"])
    o["w_bout"] = inputs["L0_attn_bout"].reshape(1, D)
    o["w_router"] = np.ascontiguousarray(inputs["L0_ffn_router_weight"].T)
    o["w_ebias"] = inputs["L0_ffn_expert_bias"].reshape(1, cfg["E"])
    o["w_sw1"] = np.ascontiguousarray(inputs["L0_ffn_sw1"].T)
    o["w_sw2"] = np.ascontiguousarray(inputs["L0_ffn_sw2"].T)
    e0 = core * cfg["EPC"]
    o["w_w1"] = np.ascontiguousarray(
        inputs["L0_ffn_w1"][e0:e0 + cfg["EPC"]])
    o["w_w2"] = np.ascontiguousarray(
        inputs["L0_ffn_w2"][e0:e0 + cfg["EPC"]])
    return o



class Em:
    COMPUTE = ("pe", "act", "dve", "pool")
    QUEUES = ("q_sp", "q_pool", "q_cc")

    def __init__(self, nc, ctx):
        self.nc = nc
        self.ctx = ctx
        self.sems = {}
        for n in ("c_pe", "c_act", "c_dve", "c_pool", "c_sp", "q_sp", "q_pool", "q_cc"):
            self.sems[n] = ctx.enter_context(nc.semaphore(n))
        self.cnt = {n: 0 for n in self.sems}
        self.q = {e: [] for e in ("pe", "act", "dve", "pool", "sp")}
        self.last_w = {}
        self.reads_since_w = {}
        self.waited = {e: {} for e in self.q}
        self.regs = {}
        self._chk = {}


    QOWNER = {"q_sp": "sp", "q_pool": "pool", "q_cc": "pool"}

    def _checkpoint(self, qsem, val):
        covered, tok = self._chk.get(qsem, (0, None))
        if covered >= val:
            return tok
        owner = self.QOWNER[qsem]
        total = self.cnt[qsem]
        csem = "c_" + owner
        self.cnt[csem] += 1
        cval = self.cnt[csem]
        qh, ch = self.sems[qsem], self.sems[csem]

        def f(h, qh=qh, total=total, ch=ch):
            h.wait_ge(qh, total).then_inc(ch, 1)

        self.q[owner].append(f)
        if self.waited[owner].get(qsem, 0) < total:
            self.waited[owner][qsem] = total
        tok = (csem, cval)
        self._chk[qsem] = (total, tok)
        return tok

    def _need_waits(self, eng, reads, writes):
        need = {}
        def add(sem, val):
            if sem.startswith("q_"):
                sem, val = self._checkpoint(sem, val)
            if val > need.get(sem, 0):
                need[sem] = val
        for rk in reads:
            w = self.last_w.get(rk)
            if w:
                add(*w)
        for rk in writes:
            w = self.last_w.get(rk)
            if w:
                add(*w)
            for r in self.reads_since_w.get(rk, ()):
                add(*r)
        out = []
        for sem, val in need.items():
            if eng == "pe" and sem == "c_pe":
                continue
            if self.waited[eng].get(sem, 0) >= val:
                continue
            out.append((sem, val))
            self.waited[eng][sem] = val
        return out

    def emit(self, eng, fn, reads=(), writes=(), dma=None, collective=False,
             skip_inc=False):
        waits = self._need_waits(eng, reads, writes)
        if dma is not None:
            sem = dma
            self.cnt[sem] += 16
            val = self.cnt[sem]
            inc = 16
        elif collective:
            sem = "q_cc"
            self.cnt[sem] += 1
            val = self.cnt[sem]
            inc = 1
        elif skip_inc:
            sem, val, inc = "c_" + eng, self.cnt["c_" + eng], 0
        else:
            sem = "c_" + eng
            self.cnt[sem] += 1
            val = self.cnt[sem]
            inc = 1
        semh = self.sems[sem]
        semobjs = self.sems

        def thunk(h, waits=waits, fn=fn, semh=semh, inc=inc):
            for (s, v) in waits:
                h.wait_ge(semobjs[s], v)
            inst = fn(h)
            if inc:
                inst.then_inc(semh, inc)

        self.q[eng].append(thunk)
        tok = (sem, val)
        for rk in writes:
            self.last_w[rk] = tok
            self.reads_since_w[rk] = []
        for rk in reads:
            self.reads_since_w.setdefault(rk, []).append(tok)
        return tok

    def wait(self, eng, sem, val):
        if self.waited[eng].get(sem, 0) >= val:
            return
        self.waited[eng][sem] = val
        semh = self.sems[sem]
        self.q[eng].append(lambda h, s=semh, v=val: h.wait_ge(s, v))

    def wait_region(self, eng, reads=(), writes=()):
        waits = self._need_waits(eng, reads, writes)
        for (s, v) in waits:
            semh = self.sems[s]
            self.q[eng].append(lambda h, ss=semh, vv=v: h.wait_ge(ss, vv))

    def raw(self, eng, fn):
        self.q[eng].append(fn)

    def barrier(self):
        snap = dict(self.cnt)
        for eng in ("pe", "act", "dve", "pool", "sp"):
            for sem, val in snap.items():
                if val > 0 and sem != "c_" + eng:
                    self.wait(eng, sem, val)

    def finalize(self, block):
        def run(h, lst):
            for f in lst:
                f(h)
        if self.q["sp"]:
            block.sync(lambda h: run(h, self.q["sp"]))
        if self.q["pool"]:
            block.gpsimd(lambda h: run(h, self.q["pool"]))
        if self.q["pe"]:
            block.tensor(lambda h: run(h, self.q["pe"]))
        if self.q["act"]:
            block.scalar(lambda h: run(h, self.q["act"]))
        if self.q["dve"]:
            block.vector(lambda h: run(h, self.q["dve"]))



def build_program(cfg):
    nc = bacc.Bacc("TRN2", target_bir_lowering=False, debug=False)
    D, E, TOPK, CAP = cfg["D"], cfg["E"], cfg["TOPK"], cfg["CAP"]
    NP, PCH, NT, NG_G = cfg["NP"], cfg["PCH"], cfg["NT"], cfg["NG_G"]
    TPC, T_TOT, LB = cfg["TPC"], cfg["T_TOT"], cfg["LB"]
    H, KVH, HD, QL, KVL = cfg["H"], cfg["KVH"], cfg["HD"], cfg["QL"], cfg["KVL"]
    FE, SHF, EPC = cfg["FE"], cfg["SHF"], cfg["EPC"]
    KPAD, KEYS = cfg["KPAD"], cfg["KEYS"]
    NB512 = TPC // 512
    stages = cfg["stages"]

    t_x = nc.dram_tensor("x_shard", [TPC, D], F32, kind="ExternalInput")
    tw = {}
    wshapes = {
        "w_qa": [D, QL], "w_qb": [QL, 2 * D], "w_kva": [D, KVL],
        "w_kvb": [KVL, 768],
        "w_out": [D, D], "w_bout": [1, D], "w_router": [D, E], "w_ebias": [1, E],
        "w_sw1": [D, SHF], "w_sw2": [SHF, D],
        "w_w1": [EPC, D, FE], "w_w2": [EPC, FE, D],
    }
    for k, s in wshapes.items():
        tw[k] = nc.dram_tensor(k, s, F32, kind="ExternalInput")
    NKT = sum(4 * PCH * (p + 1) // 128 for p in range(NP))
    cshapes = {
        "c_ident": (BF16, [128, 128]),
        "c_u128": (BF16, [128, 128]), "c_ug": (BF16, [NG_G, NG_G]),
        "c_ones1": (BF16, [128, 128]), "c_ones1f": (F32, [128, 128]),
        "c_onescf": (F32, [128, 1]), "c_u128f": (F32, [128, 128]),
        "c_iotaE": (F32, [128, E]), "c_dec64": (F32, [128, E]),
        "c_junk6": (F32, [128, TOPK]), "c_zero6": (F32, [128, TOPK]),
        "c_stmask": (BF16, [128, 8 * PCH]),
        "k_agidx": (F32, [128, NT]),
        "k_cos": (BF16, [128, TPC]), "k_sin": (BF16, [128, TPC]),
        "k_valcol": (BF16, [128, NKT]),
    }
    for k, (dt, s) in cshapes.items():
        tw[k] = nc.dram_tensor(k, s, dt, kind="ExternalInput")

    t_out = nc.dram_tensor("x_out", [TPC, D], F32, kind="ExternalOutput")

    dbg_drams = []
    def dram(name, shape, dt, **kw):
        t = nc.dram_tensor(name, shape, dt, kind="Internal", **kw)
        if name in cfg["debug"]:
            dbg_drams.append((name, t, shape, dt))
        return t
    d_agln_in = dram("agln_in", [TPC, D], BF16)
    d_agln_out = dram("agln_out", [T_TOT, D], BF16, addr_space="Shared")
    d_agcnt_in = dram("agcnt_in", [NT, E], BF16)
    d_agcnt_out = dram("agcnt_out", [NG_G, E], BF16)
    d_bases = dram("basesbuf", [NG_G, E], F32)
    d_tilebufs = [dram("tilebuf%d" % i, [LB, 1], F32) for i in range(NT)]
    d_ar_in = dram("ar_in", [128, LB // 128], F32)
    d_ar_out = dram("ar_out", [LB, 1], F32)
    d_agy_in = dram("agy_in", [EPC * CAP, D], BF16)
    d_agy_out = dram("agy_out", [E * CAP, D], BF16, addr_space="Shared")
    d_agk_in = dram("agk_in", [2 * KVH * 32, TPC], BF16)
    d_agk_out = dram("agk_out", [4 * 2 * KVH * 32, TPC], BF16)
    d_agv_in = dram("agv_in", [TPC, KVH * HD], BF16)
    d_agv_out = dram("agv_out", [4 * TPC, KVH * HD], BF16)
    NKTM = 4 * (PCH // 128) * NP
    d_kbufC = dram("kbufC", [KVH, KEYS // PCH, 64, PCH], BF16)
    d_vbufC = dram("vbufC", [KVH, KEYS // 128, 128, HD], BF16)
    d_kbufE = dram("kbufE", [KVH, 64, NKTM * 128], BF16)
    d_kbufO = dram("kbufO", [KVH, 64, NKTM * 128], BF16)
    d_vbufE = dram("vbufE", [KVH, 128, NKTM, HD], BF16)
    d_vbufO = dram("vbufO", [KVH, 128, NKTM, HD], BF16)

    ctx = ExitStack()
    with ctx:
        em = Em(nc, ctx)
        sb = lambda n, s, dt=F32: ctx.enter_context(nc.sbuf_tensor(n, s, dt))
        psum = lambda n, s, dt=F32: ctx.enter_context(nc.psum_tensor(n, s, dt))

        s_x = sb("s_x", [128, NT, D])
        s_lnfm = sb("s_lnfm", [128, D // 128, TPC // 2], BF16)
        s_lnroll = sb("s_lnroll", [128, 2, D], BF16)
        s_stats = sb("s_stats", [128, 2, NT])
        s_scr2 = sb("s_scr2", [128, 8])
        s_scores = sb("s_scores", [128, NT, E])
        s_mask = sb("s_mask", [128, NT, E])
        s_probs = s_scores
        s_pos = sb("s_pos", [128, E])
        s_top8 = sb("s_top8", [128, 8])
        s_e6 = sb("s_e6", [128, TOPK])
        s_pos6 = sb("s_pos6", [128, NT, TOPK])
        s_w6 = sb("s_w6", [128, NT, TOPK])
        s_slot6u = sb("s_slot6u", [128, NT, TOPK], U32)
        s_val6u = sb("s_val6u", [128, TOPK], U32)
        s_y6u = sb("s_y6u", [128, NT, TOPK], U32)
        s_cntg = sb("s_cntg", [NG_G, E], BF16)
        s_baserows = sb("s_baserows", [1, NT * E])
        s_idxf = sb("s_idxf", [128, 24])
        s_idxu = sb("s_idxu", [128, 24], U32)
        s_merge = sb("s_merge", [128, LB // 128])
        s_acc = sb("s_acc", [128, D])
        s_scr = s_acc
        s_ytile = sb("s_ytile", [128, D], BF16)
        s_wqa = sb("s_wqa", [128, D // 128, QL], BF16)
        s_wqb = sb("s_wqb", [QL, 2 * D], BF16)
        s_wkva = sb("s_wkva", [128, D // 128, KVL], BF16)
        s_wkvb = sb("s_wkvb", [KVL, 768], BF16)
        s_wr = sb("s_wr", [128, D // 128, E], BF16)
        NVON = (4 * PCH * NP // 128) * 65
        arA = sb("arA", [128, max(2 * 8192, 8192 + NVON + 2048 + 8 * PCH)], BF16)
        arB = sb("arB", [128, 16384], BF16)
        s_sw1 = arA[:, 0:8192].rearrange("p (a b) -> p a b", a=D // 128)
        s_sw2 = arA[:, 8192:16384].rearrange("p (a b) -> p a b", a=SHF // 128)
        s_hmid = arB[:, 0:8192].rearrange("p (a b) -> p a b", a=SHF // 128)
        s_w1e = arB[:, 8192:12288].rearrange("p (a b) -> p a b", a=D // 128)
        s_w2e = arB[:, 12288:16384].rearrange("p (a b) -> p a b", a=FE // 128)
        s_k = arA[:, 0:8192]
        s_von = arA[:, 8192:8192 + NVON]
        s_expw = arA[:, 8192 + NVON:8192 + NVON + 2048]
        s_cat = arA[:, 8192 + NVON + 2048:8192 + NVON + 2048 + 8 * PCH].rearrange(
            "p (a b) -> p a b", a=8)
        oq2 = 8 * PCH
        owout = oq2 + H * PCH
        oqa = owout + 2048
        okva = oqa + TPC
        orec = okva + H * PCH
        obcd = orec + 1024
        assert obcd + 2 * PCH <= 16384
        s_q = arB[:, 0:8 * PCH]
        s_q2 = arB[:, oq2:oq2 + H * PCH]
        s_wout = arB[:, owout:owout + 2048].rearrange(
            "p (a b) -> p a b", a=2)
        s_qa = arB[0:QL, oqa:oqa + TPC]
        s_kva = arB[0:KVL, okva:okva + TPC]
        s_catA = arB[0:64, okva:okva + H * PCH]
        s_rec = arB[:, orec:orec + 1024].bitcast(F32)
        s_bcd = arB[0:64, obcd:obcd + 2 * PCH]
        s_h1 = sb("s_h1", [128, FE // 128, CAP], BF16)
        s_xs = sb("s_xs", [128, D // 128, CAP], BF16)
        s_xstok = sb("s_xstok", [128, D], BF16)
        s_ident = sb("s_ident", [128, 128], BF16)
        s_ug = sb("s_ug", [NG_G, NG_G], BF16)
        s_ones1 = sb("s_ones1", [128, 128], BF16)
        s_ones1f = sb("s_ones1f", [128, 128])
        s_onescf = sb("s_onescf", [128, 1])
        s_u128f = sb("s_u128f", [128, 128])
        s_iotaE = sb("s_iotaE", [128, E])
        s_dec64 = sb("s_dec64", [128, E])
        s_junk6 = sb("s_junk6", [128, TOPK])
        s_zero6 = sb("s_zero6", [128, TOPK])
        s_stmask = sb("s_stmask", [128, 8 * PCH], BF16)
        s_agidx = sb("s_agidx", [128, NT])
        s_cos = sb("s_cos", [128, TPC], BF16)
        s_sin = sb("s_sin", [128, TPC], BF16)
        s_valcol = sb("s_valcol", [128, NKT], BF16)
        s_ebias_bc = sb("s_ebias_bc", [128, E])
        s_ebrow = arB[0:1, 1024:1024 + E]
        s_boutrow = arB[0:1, 0:D]
        s_bout_bc = sb("s_bout_bc", [128, D], BF16)
        s_zeros = sb("s_zeros", [128, 200])
        s_eps = sb("s_eps", [128, 1])
        s_oct = sb("s_oct", [128, 1])

        G = psum("G", [128, 2048])
        A1 = psum("A1", [128, 512])
        A2 = psum("A2", [128, 512])
        A3 = psum("A3", [128, 512])
        A4 = psum("A4", [128, 512])

        block = ctx.enter_context(nc.Block())

        def dma(eng, q, out, in_, reads, writes, **kw):
            return em.emit(eng, lambda h: h.dma_start(out=out, in_=in_, **kw),
                           reads=reads, writes=writes, dma=q)

        def setup_regs(eng, etype):
            def f(h):
                pid = nc.partition_id([etype])
                cc = pid % 4
                em.regs[eng] = {
                    "pid": pid, "cc": cc,
                    "gte": (pid // 4) * (32 * TPCH) + cc * TPCH,
                    "gto": (pid // 4) * (32 * TPCH) + (7 - cc) * TPCH,
                    "ivar": pid * (EPC * CAP),
                    "cc3": 3 - cc,
                    "kve128": cc * TPCH, "kvo128": (3 - cc) * TPCH,
                }
            em.raw(eng, f)

        def dyn_off(h, eng, terms, const, maxv=1 << 24):
            r = em.regs[eng]
            expr = None
            for regname, coef in terms:
                t = r[regname] * coef
                expr = t if expr is None else expr + t
            return expr + const

        const_loads = [
            ("c_ident", s_ident), ("c_u128f", s_u128f),
            ("c_ones1f", s_ones1f), ("c_onescf", s_onescf),
            ("c_ug", s_ug), ("c_ones1", s_ones1), ("c_iotaE", s_iotaE),
            ("c_dec64", s_dec64), ("c_junk6", s_junk6), ("c_zero6", s_zero6),
            ("c_stmask", s_stmask),
            ("k_agidx", s_agidx), ("k_cos", s_cos), ("k_sin", s_sin),
            ("k_valcol", s_valcol),
        ]
        for name, dst in const_loads:
            dma("sp", "q_sp", dst[:], tw[name].ap(), reads=(), writes=(name,))
        dma("pool", "q_pool", s_ebrow[:], tw["w_ebias"].ap(), (), ("w_ebrow",))
        dma("pool", "q_pool", s_boutrow[:], tw["w_bout"].ap(), (), ("w_brow",))
        dma("pool", "q_pool", s_wqa[:],
            tw["w_qa"].ap().rearrange("(a p) q -> p a q", p=128), (), ("w_qa",))
        dma("pool", "q_pool", s_wqb[:], tw["w_qb"].ap(), (), ("w_qb",))
        dma("pool", "q_pool", s_wkva[:],
            tw["w_kva"].ap().rearrange("(a p) q -> p a q", p=128), (), ("w_kva",))
        dma("pool", "q_pool", s_wkvb[:], tw["w_kvb"].ap(), (), ("w_kvb",))
        dma("pool", "q_pool", s_wr[:],
            tw["w_router"].ap().rearrange("(a p) q -> p a q", p=128), (), ("w_r",))
        dma("sp", "q_sp", s_x[:], t_x.ap().rearrange("(a p) d -> p a d", p=128),
            (), tuple("x%d" % i for i in range(NT)))
        em.emit("dve", lambda h: h.memset(s_zeros[:], 0.0), (), ("zeros",))
        em.emit("dve", lambda h: h.memset(s_eps[:], float(cfg["EPS"])), (), ("eps",))
        em.emit("dve", lambda h: h.memset(s_oct[:], float(HD) ** -0.5), (), ("oct",))

        em.emit("pe", lambda h: h.matmul(G[:, 0:E], s_ones1[0:1, :], s_ebrow[:],
                                         start=True, stop=True),
                reads=("c_ones1", "w_ebrow"), writes=("gb0",))
        em.emit("dve", lambda h: h.tensor_copy(out=s_ebias_bc[:], in_=G[:, 0:E]),
                reads=("gb0",), writes=("ebias_bc",))
        for half in range(2):
            em.emit("pe", lambda h, hf=half: h.matmul(
                G[:, 512 + hf * 512: 1024 + hf * 512], s_ones1[0:1, :],
                s_boutrow[0:1, hf * 512:(hf + 1) * 512], start=True, stop=True),
                reads=("c_ones1", "w_brow"), writes=("gb%d" % (1 + half),))
            em.emit("dve", lambda h, hf=half: h.tensor_copy(
                out=s_bout_bc[:, hf * 512:(hf + 1) * 512],
                in_=G[:, 512 + hf * 512:1024 + hf * 512]),
                reads=("gb%d" % (1 + half),), writes=("bout_bc",))

        def layer_norm_half(round_id, half, ag=True):
            t0, t1 = half * (NT // 2), (half + 1) * (NT // 2)
            for i in range(t0, t1):
                xt = s_x[:, i, :]
                em.emit("dve", lambda h, xt=xt, i=i: h.tensor_scalar(
                    out=s_scr[:, 0:D], in0=xt, scalar1=0.0, scalar2=None,
                    op0=ALU.add, op1=ALU.add, accum_out=s_scr2[:, 0:1]),
                    reads=("x%d" % i, "scr"), writes=("scr", "sumA"))
                em.emit("dve", lambda h, xt=xt: h.scalar_tensor_tensor(
                    out=s_scr[:, 0:D], in0=xt, scalar=1.0, in1=xt,
                    op0=ALU.mult, op1=ALU.mult, accum_out=s_scr2[:, 1:2]),
                    reads=("x%d" % i, "scr"), writes=("scr", "sumB"))
                em.emit("dve", lambda h, i=i: h.tensor_scalar(
                    out=s_stats[:, 0, i:i + 1], in0=s_scr2[:, 0:1],
                    scalar1=-1.0 / D, scalar2=None, op0=ALU.mult),
                    reads=("sumA",), writes=("negmean",))
                em.emit("dve", lambda h, i=i: h.scalar_tensor_tensor(
                    out=s_scr2[:, 2:3], in0=s_stats[:, 0, i:i + 1], scalar=1.0,
                    in1=s_stats[:, 0, i:i + 1], op0=ALU.mult, op1=ALU.mult),
                    reads=("negmean",), writes=("m2",))
                em.emit("dve", lambda h: h.scalar_tensor_tensor(
                    out=s_scr2[:, 3:4], in0=s_scr2[:, 1:2], scalar=1.0 / D,
                    in1=s_scr2[:, 2:3], op0=ALU.mult, op1=ALU.subtract),
                    reads=("sumB", "m2"), writes=("var",))
                em.emit("act", lambda h: h.activation(
                    out=s_scr2[:, 4:5], in_=s_scr2[:, 3:4], func=AF.Sqrt,
                    bias=s_eps[:, 0:1]), reads=("var", "eps"), writes=("std",))
                em.emit("dve", lambda h, i=i: h.reciprocal(
                    s_stats[:, 1, i:i + 1], s_scr2[:, 4:5]),
                    reads=("std",), writes=("rstd",))
                em.emit("dve", lambda h, xt=xt, i=i: h.tensor_scalar(
                    out=s_lnroll[:, i % 2, :], in0=xt,
                    scalar1=s_stats[:, 0, i:i + 1], scalar2=s_stats[:, 1, i:i + 1],
                    op0=ALU.add, op1=ALU.mult),
                    reads=("x%d" % i, "negmean", "rstd"),
                    writes=("lnroll%d" % (i % 2),))
                if ag:
                    dma("sp", "q_sp",
                        d_agln_in.ap()[i * 128:(i + 1) * 128, :],
                        s_lnroll[:, i % 2, :],
                        reads=("lnroll%d" % (i % 2),), writes=("aglnin",))
                for dch in range(D // 128):
                    em.emit("pe", lambda h, i=i, dch=dch: h.transpose(
                        A4[:].bitcast(BF16)[:, 0:128],
                        s_lnroll[:, i % 2, dch * 128:(dch + 1) * 128], s_ident[:]),
                        reads=("lnroll%d" % (i % 2), "c_ident"), writes=("pt",))
                    em.emit("dve", lambda h, i=i, dch=dch, t0=t0: h.tensor_copy(
                        out=s_lnfm[:, dch, (i - t0) * 128:(i - t0 + 1) * 128],
                        in_=A4[:].bitcast(BF16)[:, 0:128]),
                        reads=("pt",), writes=("lnfm",))

        Gb = lambda b, w=512: G[:, b * 512:b * 512 + w]
        TPCH = PCH // 128
        NHALF = NT // 2
        NDCH = D // 128
        scr16 = s_scr[:].bitcast(BF16)

        def router_half(rnd, half):
            t0 = half * NHALF
            for i in range(t0, t0 + NHALF):
                for dch in range(NDCH):
                    em.emit("pe", lambda h, i=i, dch=dch, t0=t0: h.matmul(
                        Gb(0, E),
                        s_lnfm[:, dch, (i - t0) * 128:(i - t0 + 1) * 128],
                        s_wr[:, dch, :],
                        start=(dch == 0), stop=(dch == NDCH - 1)),
                        reads=("lnfm", "w_r"), writes=("G0",))
                em.emit("act", lambda h, i=i: h.activation(
                    out=s_scores[:, i, :], in_=Gb(0, E), func=AF.Sigmoid),
                    reads=("G0",), writes=("sc%d" % i,))

        def shared_half(rnd, half):
            t0 = half * NHALF
            nblk = NHALF // 4
            for blk in range(nblk):
                boff = blk * 512
                for fhalf in range(2):
                    dma("pool", "q_pool", s_sw1[:],
                        tw["w_sw1"].ap().rearrange("(a p) f -> p a f", p=128)
                        [:, :, fhalf * (SHF // 2):(fhalf + 1) * (SHF // 2)],
                        reads=(), writes=("sw1",))
                    for fc in range(SHF // 2 // 128):
                        for dch in range(NDCH):
                            em.emit("pe", lambda h, fc=fc, dch=dch, boff=boff:
                                    h.matmul(A1[:],
                                             s_sw1[:, dch, fc * 128:(fc + 1) * 128],
                                             s_lnfm[:, dch, boff:boff + 512],
                                             start=(dch == 0),
                                             stop=(dch == NDCH - 1)),
                                    reads=("sw1", "lnfm"), writes=("A1",))
                        em.emit("act", lambda h, fc=fc, fhalf=fhalf: h.activation(
                            out=s_hmid[:, fhalf * 8 + fc, :], in_=A1[:],
                            func=AF.Gelu_apprx_tanh),
                            reads=("A1",), writes=("hmid",))
                for dh in range(2):
                    dma("pool", "q_pool", s_sw2[:],
                        tw["w_sw2"].ap().rearrange("(a p) f -> p a f", p=128)
                        [:, :, dh * 512:(dh + 1) * 512],
                        reads=(), writes=("sw2",))
                    for ts_ in range(4):
                        xt = t0 + blk * 4 + ts_
                        for fc in range(SHF // 128):
                            em.emit("pe", lambda h, fc=fc, ts_=ts_: h.matmul(
                                A2[:], s_hmid[:, fc, ts_ * 128:(ts_ + 1) * 128],
                                s_sw2[:, fc, :],
                                start=(fc == 0), stop=(fc == SHF // 128 - 1)),
                                reads=("hmid", "sw2"), writes=("A2",))
                        em.emit("dve", lambda h, xt=xt, dh=dh:
                                h.scalar_tensor_tensor(
                                    out=s_x[:, xt, dh * 512:(dh + 1) * 512],
                                    in0=A2[:], scalar=0.5,
                                    in1=s_x[:, xt, dh * 512:(dh + 1) * 512],
                                    op0=ALU.mult, op1=ALU.add),
                                reads=("A2", "x%d" % xt), writes=("x%d" % xt,))

        def routing(rnd):
            for i in range(NT):
                em.emit("dve", lambda h, i=i: h.tensor_tensor(
                    out=s_scr[:, 0:E], in0=s_scores[:, i, :], in1=s_ebias_bc[:],
                    op=ALU.add), reads=("sc%d" % i, "ebias_bc", "scr"),
                    writes=("scr",))
                em.emit("dve", lambda h: h.max(out=s_top8[:], in_=s_scr[:, 0:E]),
                        reads=("scr",), writes=("top8",))
                em.emit("dve", lambda h, i=i: h.tensor_scalar(
                    out=s_mask[:, i, :], in0=s_scr[:, 0:E],
                    scalar1=s_top8[:, TOPK - 1:TOPK], scalar2=None,
                    op0=ALU.is_ge), reads=("scr", "top8"), writes=("mk%d" % i,))
                em.emit("dve", lambda h, i=i: h.scalar_tensor_tensor(
                    out=s_probs[:, i, :], in0=s_scores[:, i, :], scalar=1.0,
                    in1=s_mask[:, i, :], op0=ALU.mult, op1=ALU.mult,
                    accum_out=s_scr2[:, 0:1]),
                    reads=("sc%d" % i, "mk%d" % i), writes=("sc%d" % i, "s2a"))
                em.emit("dve", lambda h: h.reciprocal(
                    s_scr2[:, 1:2], s_scr2[:, 0:1]), reads=("s2a",),
                    writes=("s2b",))
                em.emit("dve", lambda h, i=i: h.tensor_scalar_mul(
                    s_probs[:, i, :], s_probs[:, i, :], s_scr2[:, 1:2]),
                    reads=("sc%d" % i, "s2b"), writes=("sc%d" % i,))
                em.emit("pe", lambda h, i=i: h.matmul(
                    Gb(1)[0:E, i:i + 1], s_mask[:, i, :], s_onescf[:],
                    start=True, stop=True),
                    reads=("mk%d" % i, "c_onescf"), writes=("G1",))
            em.emit("dve", lambda h: h.memset(scr16[:, 0:128], 0.0),
                    reads=("scr",), writes=("scr",))
            em.emit("dve", lambda h: h.tensor_copy(
                out=scr16[0:E, 0:NT], in_=Gb(1)[0:E, 0:NT]),
                reads=("G1", "scr"), writes=("scr",))
            em.emit("pe", lambda h: h.transpose(
                A4[:].bitcast(BF16)[:, 0:128], scr16[:, 0:128], s_ident[:]),
                reads=("scr", "c_ident"), writes=("pt",))
            em.emit("dve", lambda h: h.tensor_copy(
                out=scr16[:, 128:256][0:NT, 0:E],
                in_=A4[:].bitcast(BF16)[0:NT, 0:E]),
                reads=("pt", "scr"), writes=("scrT",))
            dma("sp", "q_sp", d_agcnt_in.ap(), scr16[:, 128:256][0:NT, 0:E],
                reads=("scrT",), writes=("agcntin",))

        def bases_and_pos(rnd):
            em.emit("pool", lambda h: h.collective_compute(
                "AllGather", ALU.bypass, replica_groups=[list(range(8))],
                ins=[d_agcnt_in.ap().opt()], outs=[d_agcnt_out.ap().opt()]),
                reads=("agcntin",), writes=("agcntout",), collective=True)
            for b in range(cfg["B"]):
                for p in range(NP):
                    for hh in range(TPCH):
                        base = (32 * b + 8 * (p // 2)) * TPCH + hh
                        if p % 2 == 0:
                            def f(h, base=base, b=b, p=p, hh=hh):
                                return h.dma_start(
                                    out=s_cntg[base:base + 3 * TPCH + 1:TPCH, :],
                                    in_=d_agcnt_out.ap().rearrange(
                                        "(r m) e -> r m e", m=NT)
                                    [4 * b:4 * b + 4, p * TPCH + hh, :])
                            em.emit("sp", f, reads=("agcntout",),
                                    writes=("cntg",), dma="q_sp")
                        else:
                            for cc2 in range(4):
                                dstr = base + (7 - cc2) * TPCH
                                def f(h, dstr=dstr, b=b, cc2=cc2, p=p, hh=hh):
                                    return h.dma_start(
                                        out=s_cntg[dstr:dstr + 1, :],
                                        in_=d_agcnt_out.ap().rearrange(
                                            "(r m) e -> r m e", m=NT)
                                        [4 * b + cc2, p * TPCH + hh, :][None, :]
                                        if False else
                                        d_agcnt_out.ap().rearrange(
                                            "(r m) e -> r m e", m=NT)
                                        [4 * b + cc2:4 * b + cc2 + 1,
                                         p * TPCH + hh, :])
                                em.emit("sp", f, reads=("agcntout",),
                                        writes=("cntg",), dma="q_sp")
            em.emit("pe", lambda h: h.matmul(
                Gb(2)[0:NG_G, 0:E], s_ug[:], s_cntg[:], start=True, stop=True),
                reads=("cntg", "c_ug"), writes=("G2",))
            em.emit("dve", lambda h: h.tensor_copy(
                out=s_scr[0:NG_G, 0:E], in_=Gb(2)[0:NG_G, 0:E]),
                reads=("G2", "scr"), writes=("scr",))
            dma("sp", "q_sp", d_bases.ap(), s_scr[0:NG_G, 0:E],
                reads=("scr",), writes=("basesd",))
            for i in range(NT):
                p_, hh = divmod(i, TPCH)
                sign = 1 if p_ % 2 == 0 else -1
                Cc = (8 * (p_ // 2) + (0 if sign > 0 else 7)) * TPCH + hh

                def f(h, p_=p_, hh=hh, sign=sign, i=i):
                    r = em.regs["sp"]
                    gvar = r["gte"] if sign > 0 else r["gto"]
                    cst = (8 * (p_ // 2)) * TPCH + hh
                    return h.dma_start(out=s_baserows[0:1, i * E:(i + 1) * E],
                                       in_=d_bases.ap()[cst:, :][ds(gvar, 1), :])
                em.emit("sp", f, reads=("basesd",), writes=("baserows",),
                        dma="q_sp")

        def pos_extract(rnd):
            for i in range(NT):
                em.emit("pe", lambda h, i=i: h.matmul(
                    Gb(3)[:, 0:E], s_u128f[:], s_mask[:, i, :],
                    start=True, stop=False),
                    reads=("mk%d" % i, "c_u128f"), writes=("G3",))
                em.emit("pe", lambda h, i=i: h.matmul(
                    Gb(3)[:, 0:E], s_ones1f[0:1, :],
                    s_baserows[0:1, i * E:(i + 1) * E],
                    start=False, stop=True),
                    reads=("baserows", "c_ones1f"), writes=("G3",))
                em.emit("dve", lambda h: h.tensor_copy(
                    out=s_pos[:], in_=Gb(3)[:, 0:E]), reads=("G3",),
                    writes=("pos",))
                em.emit("dve", lambda h, i=i: h.tensor_tensor(
                    out=s_scr[:, 0:E], in0=s_mask[:, i, :], in1=s_dec64[:],
                    op=ALU.mult), reads=("mk%d" % i, "c_dec64", "scr"),
                    writes=("scr",))
                em.emit("dve", lambda h: h.max(out=s_top8[:], in_=s_scr[:, 0:E]),
                        reads=("scr",), writes=("top8",))
                em.emit("dve", lambda h: h.tensor_scalar(
                    out=s_e6[:], in0=s_top8[:, 0:TOPK], scalar1=-1.0,
                    scalar2=float(E), op0=ALU.mult, op1=ALU.add),
                    reads=("top8",), writes=("e6",))
                for k in range(TOPK):
                    em.emit("dve", lambda h, k=k: h.tensor_scalar(
                        out=s_scr[:, 64:128], in0=s_iotaE[:],
                        scalar1=s_e6[:, k:k + 1], scalar2=None,
                        op0=ALU.is_equal),
                        reads=("e6", "c_iotaE", "scr"), writes=("scr",))
                    em.emit("dve", lambda h, i=i, k=k: h.scalar_tensor_tensor(
                        out=s_scr[:, 128:192], in0=s_pos[:], scalar=1.0,
                        in1=s_scr[:, 64:128], op0=ALU.mult, op1=ALU.mult,
                        accum_out=s_pos6[:, i, k:k + 1]),
                        reads=("pos", "scr"), writes=("scr", "p6%d" % i))
                    em.emit("dve", lambda h, i=i, k=k: h.scalar_tensor_tensor(
                        out=s_scr[:, 128:192], in0=s_probs[:, i, :], scalar=0.5,
                        in1=s_scr[:, 64:128], op0=ALU.mult, op1=ALU.mult,
                        accum_out=s_w6[:, i, k:k + 1]),
                        reads=("sc%d" % i, "scr"), writes=("scr", "w6%d" % i))
                em.emit("dve", lambda h, i=i: h.tensor_scalar(
                    out=s_scr[:, 192:192 + TOPK], in0=s_pos6[:, i, :],
                    scalar1=float(CAP), scalar2=None, op0=ALU.is_lt),
                    reads=("p6%d" % i, "scr"), writes=("scr", "val6")),
                em.emit("dve", lambda h, i=i: h.tensor_scalar(
                    out=s_val6u[:], in0=s_pos6[:, i, :],
                    scalar1=float(CAP), scalar2=None, op0=ALU.is_lt),
                    reads=("p6%d" % i,), writes=("val6u",)),
                em.emit("dve", lambda h, i=i: h.tensor_tensor(
                    out=s_w6[:, i, :], in0=s_w6[:, i, :],
                    in1=s_scr[:, 192:192 + TOPK], op=ALU.mult),
                    reads=("w6%d" % i, "val6"), writes=("w6%d" % i,))
                em.emit("dve", lambda h, i=i: h.scalar_tensor_tensor(
                    out=s_scr[:, 200:200 + TOPK], in0=s_e6[:],
                    scalar=float(CAP), in1=s_pos6[:, i, :],
                    op0=ALU.mult, op1=ALU.add),
                    reads=("e6", "p6%d" % i, "scr"), writes=("scr", "slotf")),
                em.emit("dve", lambda h: h.tensor_copy(
                    out=s_scr[:, 208:208 + TOPK], in_=s_junk6[:]),
                    reads=("c_junk6", "scr"), writes=("scr",))
                em.emit("dve", lambda h: h.copy_predicated(
                    out=s_scr[:, 208:208 + TOPK], mask=s_val6u[:],
                    data=s_scr[:, 200:200 + TOPK]),
                    reads=("val6u", "slotf", "scr"), writes=("scr",))
                em.emit("dve", lambda h, i=i: h.tensor_copy(
                    out=s_slot6u[:, i, :], in_=s_scr[:, 208:208 + TOPK]),
                    reads=("scr",), writes=("sl%d" % i,))
                em.emit("dve", lambda h: h.tensor_copy(
                    out=s_scr[:, 216:216 + TOPK], in_=s_zero6[:]),
                    reads=("c_zero6", "scr"), writes=("scr",))
                em.emit("dve", lambda h: h.copy_predicated(
                    out=s_scr[:, 216:216 + TOPK], mask=s_val6u[:],
                    data=s_scr[:, 200:200 + TOPK]),
                    reads=("val6u", "slotf", "scr"), writes=("scr",))
                em.emit("dve", lambda h, i=i: h.tensor_copy(
                    out=s_y6u[:, i, :], in_=s_scr[:, 216:216 + TOPK]),
                    reads=("scr",), writes=("y6%d" % i,))

        def scatter_pairs(rnd):
            for k in range(TOPK):
                for i in range(NT):
                    em.emit("pool", lambda h, i=i, k=k: h.indirect_dma_start(
                        out=d_tilebufs[i].ap(),
                        out_offset=IndirectOffsetOnAxis(s_slot6u[:, i, k:k + 1], 0),
                        in_=s_agidx[:, i:i + 1], in_offset=None),
                        reads=("sl%d" % i, "k_agidx", "tb%d" % i),
                        writes=("tb%d" % i,), dma="q_pool")
            W = LB // 128
            for i in range(NT):
                col = 256 * (i % 2)
                dma("sp", "q_sp", s_scr[:, col:col + W],
                    d_tilebufs[i].ap().rearrange("(p f) o -> p (f o)", p=128),
                    reads=("tb%d" % i, "scrm%d" % (i % 2)),
                    writes=("scrm%d" % (i % 2),))
                if i == 0:
                    em.emit("dve", lambda h, col=col: h.tensor_copy(
                        out=s_merge[:], in_=s_scr[:, col:col + W]),
                        reads=("scrm0",), writes=("mrg",))
                else:
                    em.emit("dve", lambda h, col=col: h.tensor_tensor(
                        out=s_merge[:], in0=s_merge[:],
                        in1=s_scr[:, col:col + W], op=ALU.add),
                        reads=("scrm%d" % (i % 2), "mrg"), writes=("mrg",))
            dma("sp", "q_sp", d_ar_in.ap(), s_merge[:], reads=("mrg",),
                writes=("arin",))
            em.emit("pool", lambda h: h.collective_compute(
                "AllReduce", ALU.add, replica_groups=[list(range(8))],
                ins=[d_ar_in.ap().opt()], outs=[d_ar_out.ap().rearrange("(p f) o -> p (f o)", p=128).opt()]),
                reads=("arin",), writes=("arout",), collective=True)

        def agln_collective(rnd):
            em.emit("pool", lambda h: h.collective_compute(
                "AllGather", ALU.bypass, replica_groups=[list(range(8))],
                ins=[d_agln_in.ap().opt()], outs=[d_agln_out.ap().opt()]),
                reads=("aglnin",), writes=("aglnout",), collective=True)

        def xs_experts(rnd):
            NST = CAP // 128
            for j in range(EPC):
                for st in range(NST):
                    def f(h, j=j, st=st):
                        ivar = em.regs["pool"]["ivar"]
                        cst = j * CAP + st * 128
                        return h.dma_start(
                            out=s_idxf[:, j * NST + st:j * NST + st + 1],
                            in_=d_ar_out.ap()[cst:, :][ds(ivar, 128), :])
                    em.emit("pool", f, reads=("arout",), writes=("idxf",),
                            dma="q_pool")
            em.emit("dve", lambda h: h.tensor_copy(
                out=s_idxu[:, 0:EPC * NST], in_=s_idxf[:, 0:EPC * NST]),
                reads=("idxf",), writes=("idxu",))
            for j in range(EPC):
                dma("pool", "q_pool", s_w1e[:],
                    tw["w_w1"].ap()[j].rearrange("(a p) f -> p a f", p=128),
                    reads=(), writes=("w1e",))
                dma("pool", "q_pool", s_w2e[:],
                    tw["w_w2"].ap()[j].rearrange("(a p) f -> p a f", p=128),
                    reads=(), writes=("w2e",))
                for st in range(NST):
                    em.emit("pool", lambda h, j=j, st=st: h.indirect_dma_start(
                        out=s_xstok[:], out_offset=None,
                        in_=d_agln_out.ap(),
                        in_offset=IndirectOffsetOnAxis(
                            s_idxu[:, j * NST + st:j * NST + st + 1], 0)),
                        reads=("idxu", "aglnout", "xstok"), writes=("xstok",),
                        dma="q_pool")
                    for dch in range(NDCH):
                        em.emit("pe", lambda h, dch=dch: h.transpose(
                            A4[:].bitcast(BF16)[:, 0:128],
                            s_xstok[:, dch * 128:(dch + 1) * 128], s_ident[:]),
                            reads=("xstok", "c_ident"), writes=("pt",))
                        em.emit("dve", lambda h, dch=dch, st=st: h.tensor_copy(
                            out=s_xs[:, dch, st * 128:(st + 1) * 128],
                            in_=A4[:].bitcast(BF16)[:, 0:128]),
                            reads=("pt",), writes=("xs",))
                for fc in range(FE // 128):
                    for dch in range(NDCH):
                        em.emit("pe", lambda h, fc=fc, dch=dch: h.matmul(
                            A1[:, 0:CAP], s_w1e[:, dch, fc * 128:(fc + 1) * 128],
                            s_xs[:, dch, :], start=(dch == 0),
                            stop=(dch == NDCH - 1)),
                            reads=("w1e", "xs"), writes=("A1",))
                    em.emit("act", lambda h, fc=fc: h.activation(
                        out=s_h1[:, fc, :], in_=A1[:, 0:CAP],
                        func=AF.Gelu_apprx_tanh), reads=("A1",), writes=("h1",))
                for st in range(NST):
                    for dh in range(2):
                        for fc in range(FE // 128):
                            em.emit("pe", lambda h, st=st, dh=dh, fc=fc: h.matmul(
                                A2[:], s_h1[:, fc, st * 128:(st + 1) * 128],
                                s_w2e[:, fc, dh * 512:(dh + 1) * 512],
                                start=(fc == 0), stop=(fc == FE // 128 - 1)),
                                reads=("h1", "w2e"), writes=("A2",))
                        em.emit("dve", lambda h, dh=dh: h.tensor_copy(
                            out=s_ytile[:, dh * 512:(dh + 1) * 512], in_=A2[:]),
                            reads=("A2",), writes=("ytile",))
                    dma("sp", "q_sp",
                        d_agy_in.ap()[j * CAP + st * 128:j * CAP + (st + 1) * 128, :],
                        s_ytile[:], reads=("ytile",), writes=("agyin",))

        def combine(rnd):
            em.emit("pool", lambda h: h.collective_compute(
                "AllGather", ALU.bypass, replica_groups=[list(range(8))],
                ins=[d_agy_in.ap().opt()], outs=[d_agy_out.ap().opt()]),
                reads=("agyin",), writes=("agyout",), collective=True)
            for i in range(NT):
                for k in range(TOPK):
                    buf = s_ytile if k % 2 == 0 else s_xstok
                    bufrk = "ytile" if k % 2 == 0 else "xstok"
                    em.emit("pool", lambda h, i=i, k=k, buf=buf:
                            h.indirect_dma_start(
                                out=buf[:], out_offset=None,
                                in_=d_agy_out.ap(),
                                in_offset=IndirectOffsetOnAxis(
                                    s_y6u[:, i, k:k + 1], 0)),
                            reads=("y6%d" % i, "agyout", bufrk), writes=(bufrk,),
                            dma="q_pool")
                    if k == 0:
                        em.emit("dve", lambda h, i=i, buf=buf: h.tensor_scalar_mul(
                            s_acc[:], buf[:], s_w6[:, i, 0:1]),
                            reads=(bufrk, "w6%d" % i, "scr"), writes=("scr",))
                    else:
                        em.emit("dve", lambda h, i=i, k=k, buf=buf:
                                h.scalar_tensor_tensor(
                                    out=s_acc[:], in0=buf[:],
                                    scalar=s_w6[:, i, k:k + 1], in1=s_acc[:],
                                    op0=ALU.mult, op1=ALU.add),
                                reads=(bufrk, "w6%d" % i, "scr"), writes=("scr",))
                em.emit("dve", lambda h, i=i: h.tensor_tensor(
                    out=s_x[:, i, :], in0=s_x[:, i, :], in1=s_acc[:],
                    op=ALU.add), reads=("x%d" % i, "scr"), writes=("x%d" % i,))

        def zero_tilebufs(rnd):
            for i in range(NT):
                dma("sp", "q_sp",
                    d_tilebufs[i].ap().rearrange("(p f) o -> p (f o)", p=128),
                    s_zeros[:, 0:LB // 128],
                    reads=("zeros", "tb%d" % i), writes=("tb%d" % i,))

        def ffn_round(rnd):
            if rnd == 0:
                em.wait_region("act", writes=("w_ebrow", "w_brow"))
                em.wait_region("pool", writes=("w_ebrow", "w_brow"))
            zero_tilebufs(rnd)
            for half in (0, 1):
                layer_norm_half(rnd, half)
                router_half(rnd, half)
                shared_half(rnd, half)
            agln_collective(rnd)
            routing(rnd)
            bases_and_pos(rnd)
            pos_extract(rnd)
            scatter_pairs(rnd)
            xs_experts(rnd)
            combine(rnd)

        KVC = KVH // 2
        GK = 2048 // (4 * PCH)
        W2 = 2 * PCH
        krst = s_ytile[:, 0:512]
        vst = s_ytile[:, 512:768]

        def attn_stagea_half(half):
            t0c = half * (TPC // 2)
            for blk in range(NHALF // 4):
                boff = blk * 512
                gcol = t0c + boff
                for dch in range(NDCH):
                    em.emit("pe", lambda h, dch=dch, boff=boff: h.matmul(
                        A1[0:QL, 0:512], s_wqa[:, dch, :],
                        s_lnfm[:, dch, boff:boff + 512],
                        start=(dch == 0), stop=(dch == NDCH - 1)),
                        reads=("w_qa", "lnfm"), writes=("A1",))
                em.emit("dve", lambda h, gcol=gcol: h.tensor_copy(
                    out=s_qa[:, gcol:gcol + 512], in_=A1[0:QL, 0:512]),
                    reads=("A1",), writes=("qa",))
                for dch in range(NDCH):
                    em.emit("pe", lambda h, dch=dch, boff=boff: h.matmul(
                        A2[0:KVL, 0:512], s_wkva[:, dch, :],
                        s_lnfm[:, dch, boff:boff + 512],
                        start=(dch == 0), stop=(dch == NDCH - 1)),
                        reads=("w_kva", "lnfm"), writes=("A2",))
                em.emit("dve", lambda h, gcol=gcol: h.tensor_copy(
                    out=s_kva[:, gcol:gcol + 512], in_=A2[0:KVL, 0:512]),
                    reads=("A2",), writes=("kvacat",))

        def kv_stageb():
            for b in range(TPC // 512):
                bc0 = b * 512
                for kc in range(KVC):
                    em.emit("pe", lambda h, kc=kc, bc0=bc0: h.matmul(
                        G[:, 0:512], s_wkvb[:, kc * 128:(kc + 1) * 128],
                        s_kva[:, bc0:bc0 + 512], start=True, stop=True),
                        reads=("w_kvb", "kvacat"), writes=("Gs0",))
                    em.emit("pe", lambda h, kc=kc, bc0=bc0: h.matmul(
                        G[:, 512:1024],
                        s_wkvb[:, 512 + kc * 128:512 + (kc + 1) * 128],
                        s_kva[:, bc0:bc0 + 512], start=True, stop=True),
                        reads=("w_kvb", "kvacat"), writes=("Gs1",))
                    em.emit("dve", lambda h, bc0=bc0: h.tensor_tensor(
                        out=s_scr[0:64, 0:512], in0=G[0:64, 0:512],
                        in1=s_cos[0:64, bc0:bc0 + 512], op=ALU.mult),
                        reads=("Gs0", "k_cos", "scr"), writes=("scr",))
                    em.emit("dve", lambda h, bc0=bc0: h.tensor_tensor(
                        out=s_scr[0:64, 512:1024], in0=G[0:64, 512:1024],
                        in1=s_sin[0:64, bc0:bc0 + 512], op=ALU.mult),
                        reads=("Gs1", "k_sin", "scr"), writes=("scr",))
                    em.emit("dve", lambda h: h.tensor_tensor(
                        out=krst[0:64, :], in0=s_scr[0:64, 0:512],
                        in1=s_scr[0:64, 512:1024], op=ALU.subtract),
                        reads=("scr",), writes=("ytile",))
                    em.emit("dve", lambda h, bc0=bc0: h.tensor_tensor(
                        out=s_scr[64:128, 0:512], in0=G[64:128, 512:1024],
                        in1=s_sin[64:128, bc0:bc0 + 512], op=ALU.mult),
                        reads=("Gs1", "k_sin", "scr"), writes=("scr",))
                    em.emit("dve", lambda h, bc0=bc0: h.tensor_tensor(
                        out=s_scr[64:128, 512:1024], in0=G[64:128, 0:512],
                        in1=s_cos[64:128, bc0:bc0 + 512], op=ALU.mult),
                        reads=("Gs0", "k_cos", "scr"), writes=("scr",))
                    em.emit("dve", lambda h: h.tensor_tensor(
                        out=krst[64:128, :], in0=s_scr[64:128, 0:512],
                        in1=s_scr[64:128, 512:1024], op=ALU.add),
                        reads=("scr",), writes=("ytile",))
                    for (sr, dr) in ((0, 2 * kc * 64), (32, (2 * kc + 1) * 64),
                                     (64, 2 * kc * 64 + 32),
                                     (96, (2 * kc + 1) * 64 + 32)):
                        dma("sp", "q_sp",
                            d_agk_in.ap()[dr:dr + 32, bc0:bc0 + 512],
                            krst[sr:sr + 32, :],
                            reads=("ytile",), writes=("agkin",))
            for i in range(NT):
                em.emit("pe", lambda h, i=i: h.matmul(
                    A1[:, 0:256], s_kva[:, i * 128:(i + 1) * 128],
                    s_wkvb[:, 256:512], start=True, stop=True),
                    reads=("kvacat", "w_kvb"), writes=("A1",))
                em.emit("act", lambda h: h.copy(out=vst[:], in_=A1[:, 0:256]),
                        reads=("A1",), writes=("ytile",))
                dma("sp", "q_sp", d_agv_in.ap()[i * 128:(i + 1) * 128, :],
                    vst[:], reads=("ytile",), writes=("agvin",))

        def attn_collectives():
            em.emit("pool", lambda h: h.collective_compute(
                "AllGather", ALU.bypass,
                replica_groups=[[0, 1, 2, 3], [4, 5, 6, 7]],
                ins=[d_agk_in.ap().opt()], outs=[d_agk_out.ap().opt()]),
                reads=("agkin",), writes=("agkout",), collective=True)
            em.emit("pool", lambda h: h.collective_compute(
                "AllGather", ALU.bypass,
                replica_groups=[[0, 1, 2, 3], [4, 5, 6, 7]],
                ins=[d_agv_in.ap().opt()], outs=[d_agv_out.ap().opt()]),
                reads=("agvin",), writes=("agvout",), collective=True)

        def kv_bufs():
            for a in range(KVH):
                dma("sp", "q_sp", d_kbufC.ap()[a][0:3, :, :],
                    s_zeros[:].bitcast(BF16)[:, 0:KPAD // 2],
                    reads=("zeros",), writes=("kbufC",))
                dma("sp", "q_sp", d_vbufC.ap()[a][0:KPAD // 128, :, :],
                    s_zeros[:].bitcast(BF16)[:, 0:KPAD // 2],
                    reads=("zeros",), writes=("vbufC",))
            for pp in range(NP):
                for rp in range(4):
                    g = glocal_of(rp, pp)
                    dma("sp", "q_sp", d_kbufC.ap()[:, 3 + g, :, :],
                        d_agk_out.ap()[rp * 256:(rp + 1) * 256,
                                       pp * PCH:(pp + 1) * PCH]
                        .rearrange("(a r) t -> a r t", a=KVH),
                        reads=("agkout",), writes=("kbufC",))
                    kt0 = (KPAD + g * PCH) // 128
                    for a in range(KVH):
                        dma("sp", "q_sp",
                            d_vbufC.ap()[a][kt0:kt0 + TPCH, :, :],
                            d_agv_out.ap()[rp * TPC + pp * PCH:
                                           rp * TPC + (pp + 1) * PCH,
                                           a * HD:(a + 1) * HD]
                            .rearrange("(kt kp) d -> kt kp d", kp=128),
                            reads=("agvout",), writes=("vbufC",))
            for a in range(KVH):
                for (dk, dv, rk, rv) in (
                        (d_kbufE, d_vbufE, "cc", "kve128"),
                        (d_kbufO, d_vbufO, "cc3", "kvo128")):
                    def fs(h, a=a, dk=dk, rk=rk):
                        sh = em.regs["sp"][rk]
                        return h.dma_start(
                            out=dk.ap()[a].rearrange(
                                "f (c t) -> c f t", t=PCH),
                            in_=d_kbufC.ap()[a][ds(sh, NKTM * 128 // PCH),
                                                :, :])
                    em.emit("sp", fs, reads=("kbufC",),
                            writes=("kbufE",), dma="q_sp")

                    def fv(h, a=a, dv=dv, rv=rv):
                        sh = em.regs["sp"][rv]
                        return h.dma_start(
                            out=dv.ap()[a].rearrange("kp kt d -> kt kp d"),
                            in_=d_vbufC.ap()[a][ds(sh, NKTM), :, :])
                    em.emit("sp", fv, reads=("vbufC",),
                            writes=("vbufE",), dma="q_sp")

        def attn_phase(p):
            pu = cfg.get("phase_upto", 99)
            T = 4 * TPCH * (p + 1)
            L = 128 * T
            pc0, pc1 = p * PCH, (p + 1) * PCH
            colbase = sum(4 * TPCH * (q + 1) for q in range(p))
            kbp = d_kbufE if p % 2 == 0 else d_kbufO
            vbp = d_vbufE if p % 2 == 0 else d_vbufO
            r2 = lambda ap: ap.rearrange("p (a b) -> p a b", a=2)
            bc2 = lambda ap: ap.rearrange("p (a b) -> p a b", a=1).to_broadcast(
                [64, 2, PCH])
            cosb0 = bc2(s_cos[0:64, pc0:pc1])
            sinb0 = bc2(s_sin[0:64, pc0:pc1])
            cosb1 = bc2(s_cos[64:128, pc0:pc1])
            sinb1 = bc2(s_sin[64:128, pc0:pc1])
            for cp in range(4):
                for (jj, off) in ((2 * cp, 0), (2 * cp + 1, PCH)):
                    em.emit("pe", lambda h, jj=jj, off=off: h.matmul(
                        A4[:, off:off + PCH],
                        s_wqb[:, jj * 128:(jj + 1) * 128],
                        s_qa[:, pc0:pc1], start=True, stop=True),
                        reads=("w_qb", "qa"), writes=("A4",))
                    em.emit("pe", lambda h, jj=jj, off=off: h.matmul(
                        A3[:, off:off + PCH],
                        s_wqb[:, D + jj * 128:D + (jj + 1) * 128],
                        s_qa[:, pc0:pc1], start=True, stop=True),
                        reads=("w_qb", "qa"), writes=("A3",))
                qc0 = 2 * cp * PCH
                em.emit("dve", lambda h: h.tensor_tensor(
                    out=r2(s_scr[0:64, 0:W2]), in0=r2(A4[0:64, 0:W2]),
                    in1=cosb0, op=ALU.mult),
                    reads=("A4", "k_cos", "scr"), writes=("scr",))
                em.emit("dve", lambda h: h.tensor_tensor(
                    out=r2(s_scr[0:64, 512:512 + W2]), in0=r2(A3[0:64, 0:W2]),
                    in1=sinb0, op=ALU.mult),
                    reads=("A3", "k_sin", "scr"), writes=("scr",))
                em.emit("dve", lambda h, qc0=qc0: h.tensor_tensor(
                    out=s_q[0:64, qc0:qc0 + W2], in0=s_scr[0:64, 0:W2],
                    in1=s_scr[0:64, 512:512 + W2], op=ALU.subtract),
                    reads=("scr",), writes=("q",))
                em.emit("dve", lambda h: h.tensor_tensor(
                    out=r2(s_scr[64:128, 0:W2]), in0=r2(A3[64:128, 0:W2]),
                    in1=sinb1, op=ALU.mult),
                    reads=("A3", "k_sin", "scr"), writes=("scr",))
                em.emit("dve", lambda h: h.tensor_tensor(
                    out=r2(s_scr[64:128, 512:512 + W2]), in0=r2(A4[64:128, 0:W2]),
                    in1=cosb1, op=ALU.mult),
                    reads=("A4", "k_cos", "scr"), writes=("scr",))
                em.emit("dve", lambda h, qc0=qc0: h.tensor_tensor(
                    out=s_q[64:128, qc0:qc0 + W2], in0=s_scr[64:128, 0:W2],
                    in1=s_scr[64:128, 512:512 + W2], op=ALU.add),
                    reads=("scr",), writes=("q",))
            if pu < 2:
                return
            sqv = s_q[:].rearrange("p (j t) -> p j t", t=PCH)
            q2v = s_q2[:].rearrange("p (hh t) -> p hh t", t=PCH)
            for (sr, dr, par) in ((0, 0, 0), (32, 0, 1), (64, 32, 0),
                                  (96, 32, 1)):
                dma("sp", "q_sp", q2v[dr:dr + 32, par::2, :],
                    sqv[sr:sr + 32, :, :], reads=("q",), writes=("q2",))
            dma("sp", "q_sp", s_q2[64:128, :], s_q2[0:64, :],
                reads=("q2",), writes=("q2",))
            if pu < 3:
                return
            von3 = s_von[:, 0:T * 65].rearrange("p (t c) -> p t c", c=65)
            for a in range(KVH):
                dma("sp", "q_sp", s_k[0:64, 0:L], kbp.ap()[a][:, 0:L],
                    reads=("kbufE",), writes=("ksb",))
                dma("sp", "q_sp", s_k[64:128, 0:L], kbp.ap()[a][:, 0:L],
                    reads=("kbufE",), writes=("ksb",))
                dma("pool", "q_pool", von3[:, :, 0:64],
                    vbp.ap()[a][:, 0:T, :],
                    reads=("vbufE",), writes=("von",))
                em.emit("dve", lambda h, colbase=colbase, T=T: h.tensor_copy(
                    out=von3[:, :, 64:65],
                    in_=s_valcol[:, colbase:colbase + T].rearrange(
                        "p (t c) -> p t c", c=1)),
                    reads=("k_valcol",), writes=("von",))
                if pu < 4:
                    continue
                for kt in range(T):
                    sl = (kt % 2) * 4 * PCH
                    ek = "expw%d" % (kt % 2)
                    for hp in (0, 1):
                        pcol = sl + 2 * hp * PCH
                        gkey = "Gs%d" % (pcol // 512)
                        ktc0 = kt * 128
                        qc = (4 * a + 2 * hp) * PCH
                        th = (pcol // 512) % 2
                        em.emit("pe", lambda h, pcol=pcol, th=th,
                                ktc0=ktc0, qc=qc: h.matmul(
                            G[:, pcol:pcol + W2],
                            s_k[64 * th:64 * th + 64, ktc0:ktc0 + 128],
                            s_q2[64 * th:64 * th + 64, qc:qc + W2],
                            start=True, stop=True,
                            tile_position=(64 * th, 0)),
                            reads=("ksb", "q2"), writes=(gkey,))
                    if pu < 5:
                        continue
                    gks = tuple(sorted({"Gs%d" % (sl // 512),
                                        "Gs%d" % ((sl + 4 * PCH - 1) // 512)}))
                    em.emit("act", lambda h, sl=sl: h.activation(
                        out=s_expw[:, sl:sl + 4 * PCH],
                        in_=G[:, sl:sl + 4 * PCH], func=AF.Exp,
                        scale=s_oct[:, 0:1]),
                        reads=gks + ("oct",), writes=(ek,))
                    if kt >= T - 2:
                        mi = (kt - (T - 2)) * 4 * PCH
                        em.emit("dve", lambda h, sl=sl, mi=mi: h.tensor_tensor(
                            out=s_expw[:, sl:sl + 4 * PCH],
                            in0=s_expw[:, sl:sl + 4 * PCH],
                            in1=s_stmask[:, mi:mi + 4 * PCH], op=ALU.mult),
                            reads=(ek, "c_stmask"), writes=(ek,))
                    if pu < 6:
                        continue
                    for hp in (0, 1):
                        Ap = A1 if hp == 0 else A2
                        apk = "A1" if hp == 0 else "A2"
                        ec0 = sl + hp * 2 * PCH
                        em.emit("pe", lambda h, Ap=Ap, kt=kt, ec0=ec0:
                                h.matmul(
                                    Ap[0:65, 0:W2], von3[:, kt, :],
                                    s_expw[:, ec0:ec0 + W2],
                                    start=(kt == 0), stop=(kt == T - 1)),
                                reads=("von", ek), writes=(apk,))
                if pu < 7:
                    continue
                for hp in (0, 1):
                    Ap = A1 if hp == 0 else A2
                    apk = "A1" if hp == 0 else "A2"
                    em.emit("dve", lambda h, Ap=Ap: h.reciprocal(
                        s_rec[64:65, 0:W2], Ap[64:65, 0:W2]),
                        reads=(apk,), writes=("rec",))
                    em.emit("pe", lambda h: h.matmul(
                        A3[0:64, 0:W2], s_ones1f[64:65, 0:64],
                        s_rec[64:65, 0:W2], start=True, stop=True,
                        tile_position=(64, 0)),
                        reads=("rec", "c_ones1f"), writes=("A3",))
                    em.emit("act", lambda h: h.copy(
                        out=s_bcd[:, 0:W2], in_=A3[0:64, 0:W2]),
                        reads=("A3",), writes=("bcd",))
                    cc0 = (4 * a + 2 * hp) * PCH
                    em.emit("dve", lambda h, Ap=Ap, cc0=cc0: h.tensor_tensor(
                        out=s_catA[:, cc0:cc0 + W2], in0=Ap[0:64, 0:W2],
                        in1=s_bcd[:, 0:W2], op=ALU.mult),
                        reads=(apk, "bcd"), writes=("kvacat",))
            if pu < 8:
                return
            catv = s_catA[:].rearrange("p (j e t) -> p j e t", e=2, t=PCH)
            dma("sp", "q_sp", s_cat[0:64, :, :], catv[:, :, 0, :],
                reads=("kvacat",), writes=("cat",))
            dma("sp", "q_sp", s_cat[64:128, :, :], catv[:, :, 1, :],
                reads=("kvacat",), writes=("cat",))
            for j in range(8):
                b = j % 2
                dma("pool", "q_pool", s_wout[:, b, :],
                    tw["w_out"].ap().rearrange("(a p) q -> p a q", p=128)
                    [:, j:j + 1, :],
                    reads=(), writes=("wout%d" % b,))
                for ts_ in range(TPCH):
                    for dh in range(2):
                        bk = ts_ * 2 + dh
                        em.emit("pe", lambda h, bk=bk, j=j, b=b, dh=dh,
                                ts_=ts_: h.matmul(
                            G[:, bk * 512:(bk + 1) * 512],
                            s_cat[:, j, ts_ * 128:(ts_ + 1) * 128],
                            s_wout[:, b, dh * 512:(dh + 1) * 512],
                            start=(j == 0), stop=(j == 7)),
                            reads=("cat", "wout%d" % b), writes=("Gs%d" % bk,))
            for ts_ in range(TPCH):
                xt = p * TPCH + ts_
                for dh in range(2):
                    bk = ts_ * 2 + dh
                    em.emit("dve", lambda h, bk=bk, dh=dh: h.tensor_tensor(
                        out=s_scr[:, 0:512], in0=G[:, bk * 512:(bk + 1) * 512],
                        in1=s_bout_bc[:, dh * 512:(dh + 1) * 512], op=ALU.add),
                        reads=("Gs%d" % bk, "bout_bc", "scr"), writes=("scr",))
                    em.emit("dve", lambda h, xt=xt, dh=dh: h.tensor_tensor(
                        out=s_x[:, xt, dh * 512:(dh + 1) * 512],
                        in0=s_x[:, xt, dh * 512:(dh + 1) * 512],
                        in1=s_scr[:, 0:512], op=ALU.add),
                        reads=("scr", "x%d" % xt), writes=("x%d" % xt,))

        def attn_round():
            upto = cfg.get("attn_upto", 99)
            for half in (0, 1):
                layer_norm_half(1, half, ag=False)
                attn_stagea_half(half)
            if upto >= 2:
                kv_stageb()
            if upto >= 3:
                attn_collectives()
            if upto >= 4:
                kv_bufs()
            for p in range(NP if upto >= 7 else max(0, upto - 4)):
                attn_phase(p)

        def dump_debug():
            specs = {
                "stats": ([128, 2 * NT], F32,
                          lambda: s_stats[:].rearrange("p a b -> p (a b)"),
                          ("negmean", "rstd")),
                "scores": ([128, NT * E], F32,
                           lambda: s_scores[:].rearrange("p a b -> p (a b)"),
                           ("scores",)),
                "mask": ([128, NT * E], F32,
                         lambda: s_mask[:].rearrange("p a b -> p (a b)"), ("mask",)),
                "probs": ([128, NT * E], F32,
                          lambda: s_probs[:].rearrange("p a b -> p (a b)"),
                          ("probs",)),
                "e6": ([128, TOPK], F32, lambda: s_e6[:], ("e6",)),
                "pos6": ([128, NT * TOPK], F32,
                         lambda: s_pos6[:].rearrange("p a b -> p (a b)"), ("pos6",)),
                "w6": ([128, NT * TOPK], F32,
                       lambda: s_w6[:].rearrange("p a b -> p (a b)"), ("w6",)),
                "cntg": ([NG_G, E], BF16, lambda: s_cntg[:], ("cntg",)),
                "baserows": ([1, NT * E], F32, lambda: s_baserows[:], ("baserows",)),
            }
            for name in cfg["debug"]:
                if name not in specs:
                    continue
                shape, dt, src, rds = specs[name]
                t = nc.dram_tensor("dbg_" + name, shape, dt, kind="ExternalOutput")
                dma("sp", "q_sp", t.ap(), src(), reads=rds, writes=("dbgout",))
            for name, t, shape, dt in dbg_drams:
                o = nc.dram_tensor("dbg_" + name, shape, dt, kind="ExternalOutput")
                em.barrier()
                dma("sp", "q_sp", o.ap(), t.ap(), reads=(), writes=("dbgout",))

        def store_out():
            dump_debug()
            dma("sp", "q_sp", t_out.ap().rearrange("(a p) d -> p a d", p=128),
                s_x[:], reads=tuple("x%d" % i for i in range(NT)), writes=("out",))
            em.wait("sp", "q_sp", em.cnt["q_sp"])
            em.wait("sp", "q_pool", em.cnt["q_pool"])
            em.wait("sp", "q_cc", em.cnt["q_cc"])
            em.wait("sp", "c_pe", em.cnt["c_pe"])
            em.wait("sp", "c_act", em.cnt["c_act"])
            em.wait("sp", "c_dve", em.cnt["c_dve"])
            em.wait("sp", "c_pool", em.cnt["c_pool"])

        setup_regs("sp", ET.SP)
        setup_regs("pool", ET.Pool)

        if not stages:
            for half in (0, 1):
                layer_norm_half(0, half)
        if "ffn1" in stages:
            ffn_round(0)
        if "attn" in stages:
            em.barrier()
            attn_round()
            em.barrier()
        if "ffn2" in stages:
            ffn_round(1)
        store_out()

        em.finalize(block)
    return nc



_CACHE = {}


def _get_program(cfg):
    key = cfg["name"] + str(cfg["stages"])
    if key not in _CACHE:
        nc = build_program(cfg)
        nc.compile()
        _CACHE[key] = nc
    return _CACHE[key]


def make_in_maps(inputs, cfg):
    consts = host_consts(cfg)
    xflat = np.ascontiguousarray(
        np.asarray(inputs["L0_attn_x"], np.float32).reshape(cfg["T_TOT"], cfg["D"]))
    in_maps = []
    for c in range(cfg["NCORE"]):
        m = {}
        m.update({k: np.ascontiguousarray(v) for k, v in consts.items()})
        m.update({k: np.ascontiguousarray(v)
                  for k, v in host_core_consts(c, cfg).items()})
        m.update({k: np.ascontiguousarray(v.astype(np.float32))
                  for k, v in host_weights(inputs, cfg, c).items()})
        m["x_shard"] = np.ascontiguousarray(xflat[core_token_rows(c, cfg)])
        in_maps.append(m)
    return in_maps


def assemble_out(results, cfg):
    out = np.zeros((cfg["T_TOT"], cfg["D"]), np.float32)
    for c in range(cfg["NCORE"]):
        out[core_token_rows(c, cfg)] = results[c]["x_out"]
    return out.reshape(cfg["B"], cfg["S"], cfg["D"])


def kernel(**inputs):
    cfg = make_cfg("full")
    nc = _get_program(cfg)
    in_maps = make_in_maps(inputs, cfg)
    res = run_bass_kernel_spmd(nc, in_maps, core_ids=list(range(cfg["NCORE"])))
    return assemble_out(res.results, cfg)
